# version 15
# baseline (speedup 1.0000x reference)
import numpy as np
import ml_dtypes
import concourse.bass as bass
import concourse.mybir as mybir
from concourse.bass_utils import run_bass_kernel_spmd
from concourse.alu_op_type import AluOpType as Op
from contextlib import ExitStack

F32 = mybir.dt.float32
BF16 = mybir.dt.bfloat16
I32 = mybir.dt.int32
AF = mybir.ActivationFunctionType
AX = mybir.AxisListType

NCORES = 8
C = 1024
NC = 8
H = 16
HD = 64
I3 = 3072
NI = 24
CMAGIC = 1.5 * 2.0 ** 23
QMAX = 32767.0
LN_EPS = 1e-5
GN_EPS = 1e-5 * 64

LD, GP, PE, DV, AC = "sync", "gp", "pe", "dve", "act"


class P:

    ENGS = (LD, GP, PE, DV, AC)
    SEMS = ("ld", "gpd", "gpc", "pe", "dve", "act")
    SEM_ENG = {"ld": LD, "gpd": GP, "gpc": GP, "pe": PE, "dve": DV, "act": AC}

    def __init__(self, nc):
        self.nc = nc
        self.items = {e: [] for e in self.ENGS}
        self.es = ExitStack()
        self.sem = {s: self.es.enter_context(nc.semaphore("s_" + s)) for s in self.SEMS}
        self.cnt = {s: 0 for s in self.SEMS}
        self.waited = {e: {} for e in self.ENGS}
        self.bank = {}
        self.fence_pend = {e: [] for e in self.ENGS}
        self.prev_small = {e: None for e in self.ENGS}
        self.prev_out = {e: None for e in self.ENGS}

    def _sem_of(self, e, dma):
        if e == GP:
            return "gpd" if dma else "gpc"
        if e == LD:
            return "ld"
        return {PE: "pe", DV: "dve", AC: "act"}[e]

    def tok(self, e):
        if e == GP:
            return [("gpd", self.cnt["gpd"]), ("gpc", self.cnt["gpc"])]
        return [(self._sem_of(e, e == LD), self.cnt[self._sem_of(e, e == LD)])]

    def _flat(self, waits):
        out = []
        for t in waits:
            if t is None:
                continue
            if isinstance(t, list):
                out.extend(x for x in t if x is not None)
            else:
                out.append(t)
        return out

    def _mkwaits(self, e, waits):
        best = {}
        for (sn, v) in self._flat(waits):
            if v <= 0 or self.SEM_ENG[sn] == e:
                continue
            if v > best.get(sn, -1):
                best[sn] = v
        wl = []
        wd = self.waited[e]
        for sn, v in best.items():
            if wd.get(sn, -1) >= v:
                continue
            wd[sn] = v
            wl.append((self.sem[sn], v))
        return wl

    def fence(self):
        snap = [(s, self.cnt[s]) for s in self.SEMS]
        for e in self.ENGS:
            self.fence_pend[e] = snap

    def op(self, e, meth, *args, waits=(), selfw=(), inc=True, dma=False, **kw):
        amt = 16 if dma else 1
        sn = self._sem_of(e, dma)
        if inc:
            self.cnt[sn] += amt
        waits = self._flat(waits)
        if self.fence_pend[e]:
            waits = waits + self.fence_pend[e]
            self.fence_pend[e] = []
        wl = self._mkwaits(e, waits)
        for (sn2, v2) in self._flat(selfw):
            if self.waited[e].get("#" + sn2, -1) < v2:
                self.waited[e]["#" + sn2] = v2
                wl.append((self.sem[sn2], v2))
        if e in (DV, AC) and self.prev_small[e] is not None:
            sn2, v2 = self.prev_small[e]
            if self.waited[e].get("#" + sn2, -1) < v2:
                self.waited[e]["#" + sn2] = v2
                wl.append((self.sem[sn2], v2))
        def _names(aps):
            s = set()
            for a_ in aps:
                try:
                    s.add(a_.tensor.name)
                except Exception:
                    pass
            return s

        cur_w = _names([args[0] if args else None, kw.get("accum_out")])
        cur_r = _names(list(args[1:]) + [kw.get("bias"), kw.get("scale")])
        if e in (DV, AC, GP) and not dma and self.prev_out.get(e) is not None:
            (sn2, v2), pw, pr = self.prev_out[e]
            if ((cur_r & pw) or (cur_w & (pw | pr))) and \
                    self.waited[e].get("#" + sn2, -1) < v2:
                self.waited[e]["#" + sn2] = v2
                wl.append((self.sem[sn2], v2))
        if e in (DV, AC, GP) and not dma and inc:
            self.prev_out[e] = ((sn, self.cnt[sn]), cur_w, cur_r)
        if e in (DV, AC) and inc:
            small = bool(kw.get("accum_out") is not None)
            if not small and meth != "engine_nop" and args:
                try:
                    a0 = args[0]
                    cnts = 1
                    for (st_, ct_) in list(a0.ap)[1:]:
                        cnts *= ct_
                    small = cnts * (2 if "bfloat" in str(a0.tensor.dtype) else 4) < 512
                except Exception:
                    small = True
            self.prev_small[e] = (sn, self.cnt[sn]) if small else None
        mysem = self.sem[sn]

        def fn(h, meth=meth, args=args, kw=kw, wl=wl, inc=inc, amt=amt):
            for (s, v) in wl:
                h.wait_ge(s, v)
            ins = getattr(h, meth)(*args, **kw)
            if inc:
                ins.then_inc(mysem, amt)
        self.items[e].append(fn)
        return (sn, self.cnt[sn])

    def waitonly(self, e, waits):
        wl = self._mkwaits(e, waits)
        if not wl:
            return

        def fn(h, wl=wl):
            for (s, v) in wl:
                h.wait_ge(s, v)
        self.items[e].append(fn)

    def finalize(self, block):
        def run(e):
            def _(h):
                for fn in self.items[e]:
                    fn(h)
            return _
        block.sync(run(LD))
        block.gpsimd(run(GP))
        block.tensor(run(PE))
        block.vector(run(DV))
        block.scalar(run(AC))


def _finish(nc, p, pes=None, es=None):
    fgp = p.cnt["gpd"]

    def fin(h, fgp=fgp, s=p.sem["gpd"]):
        h.wait_ge(s, fgp)
    p.items[GP].append(fin)
    with nc.Block() as block:
        p.finalize(block)
    if pes is not None:
        pes.close()
    if es is not None:
        es.close()
    return nc


def build_program(bpc=4, T=2048, dbg=False, upto=None, dbgout=None):
    TOK = bpc * T
    TILE = 512
    NTIL = TOK // TILE
    NST = TOK // 128
    TPS = T // TILE
    CPT = TILE // 128

    nc = bass.Bass("TRN2", target_bir_lowering=False, debug=False)

    def din(name, shape, dt=F32):
        return nc.dram_tensor(name, list(shape), dt, kind="ExternalInput").ap()

    x_tm = din("x_tm", (TOK, C))
    x_fm = din("x_fm", (C, TOK))
    w5 = din("w5", (5 * C, C))
    maa_w1 = din("maa_w1", (C, 160))
    maa_w2 = din("maa_w2", (128, 5 * C))
    td_w1 = din("td_w1", (C, 64))
    td_w2 = din("td_w2", (64, C))
    vecs = din("vecs", (C, 8))
    lnwb = din("lnwb", (128, 2 * C), BF16)
    identf = din("identf", (128, 128))
    identb = din("identb", (128, 128), BF16)
    mask4 = din("mask4", (128, 512), BF16)
    onesf = din("onesf", (128, 128))
    bind_up = din("bind_up", (32, NC * 128))
    bind_dn = din("bind_dn", (96, NI * 128))
    cconst = din("cconst", (128, 2))
    wup_n = din("wup_n", (I3, C))
    wgate_n = din("wgate_n", (I3, C))
    wdown_n = din("wdown_n", (C, I3))
    wupT = din("wupT", (C, I3))
    wgateT = din("wgateT", (C, I3))
    wdownT = din("wdownT", (I3, C))

    out_fm = nc.dram_tensor("out_fm", [C, TOK], F32, kind="ExternalOutput").ap()

    dk = "ExternalOutput" if dbg else "Internal"
    sp_xn = nc.dram_tensor("sp_xn", [C, TOK], BF16, kind=dk).ap()
    sp_xx = nc.dram_tensor("sp_xx", [C, TOK], BF16, kind=dk).ap()
    sp_w5 = nc.dram_tensor("sp_w5", [5 * C, C], BF16).ap()
    sp_wupT = nc.dram_tensor("sp_wupT", [C, I3], BF16, kind=dk).ap()
    sp_wgateT = nc.dram_tensor("sp_wgateT", [C, I3], BF16).ap()
    sp_wdownT = nc.dram_tensor("sp_wdownT", [I3, C], BF16).ap()
    sp_up = nc.dram_tensor("sp_up", [I3, TOK], BF16, kind=dk).ap()
    sp_gate = nc.dram_tensor("sp_gate", [I3, TOK], BF16, kind=dk).ap()
    sp_g2p = nc.dram_tensor("sp_g2p", [I3, TOK], BF16).ap()
    sp_opre = nc.dram_tensor("sp_opre", [I3, TOK], BF16, kind=dk).ap()
    if dbg:
        dbg_att = nc.dram_tensor("dbg_att", [C, TOK], F32, kind="ExternalOutput").ap()
        dbg_sc = nc.dram_tensor("dbg_sc", [128, 16], F32, kind="ExternalOutput").ap()

    es = ExitStack()

    def sb(name, shape, dt=F32):
        return es.enter_context(nc.sbuf_tensor(name, list(shape), dt))

    cs = sb("c_vecs", (128, NC * 8))
    c_lnw = sb("c_lnw", (128, C), BF16)
    c_lnb = sb("c_lnb", (128, C), BF16)
    c_idf = sb("c_idf", (128, 128))
    c_idb = sb("c_idb", (128, 128), BF16)
    c_mask = sb("c_mask", (128, 512), BF16)
    c_ones = sb("c_ones", (128, 128))
    c_cc = sb("c_cc", (128, 2))
    scl = sb("scl", (128, 16))
    mw1 = sb("mw1", (128, NC * 160), BF16)
    mw2 = sb("mw2", (128, 5 * C), BF16)
    tw1 = sb("tw1", (128, NC * 64), BF16)
    tw2 = sb("tw2", (64, C), BF16)
    SS = sb("SS", (128, NC * 128))

    pes = ExitStack()
    pf = [pes.enter_context(nc.psum_tensor("pf%d" % i, [128, 512], F32)) for i in range(6)]
    pb16 = [pes.enter_context(nc.psum_tensor("pb%d" % i, [128, 1024], BF16)) for i in range(2)]

    p = P(nc)

    p.chase_tok = None
    p.gp_chase_tok = None

    def dma_in(dst, src, waits=()):
        sw = [p.chase_tok] if p.chase_tok else []
        p.chase_tok = None
        return p.op(LD, "dma_start", out=dst, in_=src, waits=waits, dma=True,
                    selfw=sw)

    def dma_out(dst, src, waits=(), accum=False):
        sw = [p.gp_chase_tok] if p.gp_chase_tok else []
        p.gp_chase_tok = None
        if accum:
            return p.op(GP, "dma_start", out=dst, in_=src, accum_op=Op.add,
                        waits=waits, dma=True, selfw=sw)
        return p.op(GP, "dma_start", out=dst, in_=src, waits=waits, dma=True,
                    selfw=sw)

    dmy = sb("dmy", (128, 2))
    dmy2 = sb("dmy2", (128, 2))

    def chase():
        t = p.op(LD, "dma_start", out=dmy[:, :], in_=cconst, dma=True,
                 selfw=[("ld", p.cnt["ld"])])
        p.chase_tok = t
        return t

    def gp_chase():
        sw = [("gpd", p.cnt["gpd"])]
        if p.gp_chase_tok:
            sw.append(p.gp_chase_tok)
        p.gp_chase_tok = None
        t = p.op(GP, "dma_start", out=dmy2[:, :], in_=cconst, dma=True,
                 selfw=sw)
        p.gp_chase_tok = t
        return t

    def fm3(dram, col0, ncols):
        return dram[:, col0:col0 + ncols].rearrange("(a p) n -> p a n", p=128)

    def chunked(ap_, k):
        return ap_.rearrange("p (a k) -> p a k", k=k)

    def bwait(key):
        return p.bank.get(key)

    def bset(key, token):
        p.bank[key] = token

    def fin_scale(mxap, mnap, stt, col_s, col_i, extra=()):
        t1 = p.op(DV, "reduce_max", stt[:, 0:1], mxap, axis=AX.X, waits=list(extra),
                  selfw=[p.tok(DV)])
        t1 = p.op(DV, "tensor_reduce", stt[:, 12:13], mnap, AX.X, Op.min)
        t1 = p.op(DV, "tensor_scalar", stt[:, 12:13], stt[:, 12:13], -1.0, None,
                  Op.mult)
        t1 = p.op(DV, "tensor_max", stt[:, 0:1], stt[:, 0:1], stt[:, 12:13])
        t2 = p.op(PE, "transpose", pf[2][0:1, 0:128], stt[:, 0:1], c_idf[:, :],
                  waits=[t1, bwait("b2")])
        t3 = p.op(DV, "reduce_max", stt[0:1, 1:2], pf[2][0:1, 0:128], axis=AX.X,
                  waits=[t2])
        t3 = p.op(DV, "tensor_scalar", stt[0:1, 2:3], stt[0:1, 1:2], 1.0 / QMAX,
                  1e-12, Op.mult, Op.max)
        t3 = p.op(DV, "reciprocal", stt[0:1, 3:4], stt[0:1, 2:3])
        tm = p.op(PE, "matmul", pf[2][:, 0:2], c_ones[0:1, :], stt[0:1, 2:4],
                  start=True, stop=True, waits=[t3])
        t4 = p.op(DV, "tensor_copy", scl[:, col_s:col_s + 1], pf[2][:, 0:1], waits=[tm])
        t4 = p.op(DV, "tensor_copy", scl[:, col_i:col_i + 1], pf[2][:, 1:2])
        bset("b2", t4)
        return t4

    def heron_rstd(stt, c_in, c_out, c_t, c_t2, wide=False, iters=5):
        w = stt[:, c_in:c_in + 1]
        s_ = stt[:, c_out:c_out + 1]
        t_ = stt[:, c_t:c_t + 1]
        t2 = stt[:, c_t2:c_t2 + 1]
        if wide:
            p.op(DV, "tensor_scalar", s_, w, 0.1, 0.95, Op.is_ge, Op.mult)
            p.op(DV, "tensor_scalar", s_, s_, 0.05, None, Op.add)
            p.op(DV, "tensor_scalar", t_, w, 10.0, 29.0, Op.is_ge, Op.mult)
            p.op(DV, "tensor_add", s_, s_, t_)
            p.op(DV, "tensor_scalar", t_, w, 1000.0, 670.0, Op.is_ge, Op.mult)
            p.op(DV, "tensor_add", s_, s_, t_)
        else:
            p.op(DV, "tensor_scalar", s_, w, 0.5, 0.5, Op.mult, Op.add)
        for _ in range(iters):
            p.op(DV, "reciprocal", t_, s_)
            p.op(DV, "tensor_mul", t_, t_, w)
            p.op(DV, "tensor_add", t_, t_, s_)
            p.op(DV, "tensor_scalar", s_, t_, 0.5, None, Op.mult)
        p.op(DV, "reciprocal", t_, s_)
        p.op(DV, "tensor_mul", t2, t_, t_)
        p.op(DV, "tensor_mul", t2, t2, w)
        p.op(DV, "tensor_scalar", t2, t2, -0.5, 1.5, Op.mult, Op.add)
        return p.op(DV, "tensor_mul", s_, t_, t2)

    tcons = [
        dma_in(chunked(cs[:, :], 8), vecs.rearrange("(a p) k -> p a k", p=128)),
        dma_in(c_lnw[:, :], lnwb[:, 0:C]),
        dma_in(c_lnb[:, :], lnwb[:, C:2 * C]),
        dma_in(c_idf[:, :], identf),
        dma_in(c_idb[:, :], identb),
        dma_in(c_mask[:, :], mask4),
        dma_in(c_ones[:, :], onesf),
    ]
    dma_in(c_cc[:, :], cconst)
    tcons = [chase()]
    p.op(DV, "memset", scl[:, :], 1.0)

    if upto == "consts":
        return _finish(nc, p, pes, es)

    with nc.sbuf_tensor("pre_f32", [128, NC * 1024], F32) as stage, \
         nc.sbuf_tensor("pre_b16", [128, NC * 1024], BF16) as stage_b:
        t1 = dma_in(chunked(stage[:, 0:NC * 160], 160),
                    maa_w1.rearrange("(a p) k -> p a k", p=128))
        t2 = p.op(DV, "tensor_copy", mw1[:, :], stage[:, 0:NC * 160],
                  waits=[t1, chase()])
        t1 = dma_in(stage[:, 0:5 * C], maa_w2, waits=[t2])
        t2 = p.op(DV, "tensor_copy", mw2[:, :], stage[:, 0:5 * C],
                  waits=[t1, chase()])
        t1 = dma_in(chunked(stage[:, 0:NC * 64], 64),
                    td_w1.rearrange("(a p) k -> p a k", p=128), waits=[t2])
        t2 = p.op(DV, "tensor_copy", tw1[:, :], stage[:, 0:NC * 64],
                  waits=[t1, chase()])
        t1 = dma_in(stage[0:64, 0:C], td_w2, waits=[t2])
        t2 = p.op(DV, "tensor_copy", tw2[:, :], stage[0:64, 0:C],
                  waits=[t1, chase()])
        tst = t2
        for wi in range(5):
            tl = dma_in(chunked(stage[:, :], 1024),
                        w5[wi * C:(wi + 1) * C, :].rearrange("(a p) n -> p a n", p=128),
                        waits=[tst])
            sc = 0.5 if wi == 4 else 1.0
            tc = p.op(DV, "tensor_scalar", stage_b[:, :], stage[:, :], sc, None,
                      Op.mult, waits=[tl, chase()])
            tst = dma_out(sp_w5[wi * C:(wi + 1) * C, :].rearrange("(a p) n -> p a n", p=128),
                          chunked(stage_b[:, :], 1024), waits=[tc])

        if upto == "w5":
            return _finish(nc, p)

        for dqi, (w_nat, w_t, spill, nko, nblk, bind, bK) in enumerate((
                (wup_n, wupT, sp_wupT, NI, 32, bind_up, 32),
                (wgate_n, wgateT, sp_wgateT, NI, 32, bind_up, 32),
                (wdown_n, wdownT, sp_wdownT, NC, 96, bind_dn, 96))):
            nfree = int(w_nat.shape[1])
            tfree = int(w_t.shape[1])
            ncc = int(w_t.shape[0]) // 128
            with nc.sbuf_tensor("dq_nat%d" % dqi, [128, nfree], F32) as natst, \
                 nc.sbuf_tensor("dq_s%d" % dqi, [128, nko * nblk], F32) as s_nat, \
                 nc.sbuf_tensor("dq_sT%d" % dqi, [bK, nko * 128], F32) as s_T, \
                 nc.sbuf_tensor("dq_bc%d" % dqi, [128, tfree], F32) as s_bc, \
                 nc.sbuf_tensor("dq_ib%d" % dqi, [128, tfree], F32) as i_bc, \
                 nc.sbuf_tensor("dq_bi%d" % dqi, [bK, ncc * 128], F32) as bi, \
                 nc.sbuf_tensor("dq_wt%d" % dqi, [128, tfree], F32) as wtst, \
                 nc.sbuf_tensor("dq_t%d" % dqi, [128, tfree], F32) as tqt, \
                 nc.sbuf_tensor("dq_o%d" % dqi, [128, tfree], BF16) as ost:
                tb = dma_in(bi[:, :], bind[:, 0:ncc * 128])
                tb = chase()
                tm = p.tok(DV)
                for ko in range(nko):
                    tl = dma_in(natst[:, :], w_nat[ko * 128:(ko + 1) * 128, :], waits=[tm])
                    tm = p.op(DV, "reduce_max", s_nat[:, ko * nblk:(ko + 1) * nblk],
                              natst[:, :].rearrange("p (w k) -> p w k", k=32),
                              axis=AX.X, apply_absolute_value=True,
                              waits=[tl, chase()])
                if upto == "dq_nat" and dqi == 0:
                    return _finish(nc, p)
                ts = p.op(DV, "tensor_scalar", s_nat[:, :], s_nat[:, :], 1.0 / 7.0,
                          1e-12, Op.mult, Op.max, waits=[tm])
                if upto == "dq_ts" and dqi == 0:
                    return _finish(nc, p)
                tev = None
                for ko in range(nko):
                    bk = "b%d" % (ko % 2)
                    tt = p.op(PE, "transpose", pf[ko % 2][0:nblk, 0:128],
                              s_nat[:, ko * nblk:(ko + 1) * nblk], c_idf[:, :],
                              waits=[ts, bwait(bk)])
                    tev = p.op(DV, "tensor_copy", s_T[0:nblk, ko * 128:(ko + 1) * 128],
                               pf[ko % 2][0:nblk, 0:128], waits=[tt])
                    bset(bk, tev)
                tsT = tev
                if upto == "dq_sT" and dqi == 0:
                    return _finish(nc, p)
                tdone = p.tok(GP)
                for cc in range(ncc):
                    tl = dma_in(wtst[:, :], w_t[cc * 128:(cc + 1) * 128, :], waits=[tdone])
                    tl = chase()
                    tbc = None
                    for jb in range(tfree // 512):
                        tmm = p.op(PE, "matmul", pf[1][:, :],
                                   bi[0:bK, cc * 128:(cc + 1) * 128],
                                   s_T[0:bK, jb * 512:(jb + 1) * 512],
                                   start=True, stop=True,
                                   waits=[tsT, tb, bwait("b1")])
                        tbc = p.op(DV, "tensor_copy", s_bc[:, jb * 512:(jb + 1) * 512],
                                   pf[1][:, :], waits=[tmm])
                        bset("b1", tbc)
                    t3 = p.op(DV, "reciprocal", i_bc[:, :], s_bc[:, :], waits=[tbc])
                    t3 = p.op(DV, "tensor_mul", tqt[:, :], wtst[:, :], i_bc[:, :],
                              waits=[tl, t3])
                    t3 = p.op(AC, "activation", tqt[:, :], tqt[:, :], AF.Identity,
                              bias=c_cc[:, 0:1], waits=[t3])
                    t3 = p.op(AC, "activation", tqt[:, :], tqt[:, :], AF.Identity,
                              bias=c_cc[:, 0:1], scale=-1.0)
                    t3 = p.op(DV, "scalar_tensor_tensor", ost[:, :], tqt[:, :],
                              -1.0, s_bc[:, :], Op.mult, Op.mult, waits=[t3])
                    tdone = dma_out(spill[cc * 128:(cc + 1) * 128, :], ost[:, :],
                                    waits=[t3])
                    if upto == "dq_cc%d" % cc and dqi == 0:
                        return _finish(nc, p)
            p.fence()
            if upto == "dq%d" % dqi:
                return _finish(nc, p)

    p.fence()

    if upto == "pre":
        return _finish(nc, p, pes, es)

    with nc.sbuf_tensor("a_x", [128, 2 * 1024], F32) as a_x, \
         nc.sbuf_tensor("a_sq", [128, 1024], F32) as a_sq, \
         nc.sbuf_tensor("a_st", [128, 16], F32) as a_st, \
         nc.sbuf_tensor("a_xnf", [128, 1024], F32) as a_xnf, \
         nc.sbuf_tensor("a_pc", [128, 8], F32) as a_pc, \
         nc.sbuf_tensor("a_xnb", [128, 2 * 1024], BF16) as a_xnb, \
         nc.sbuf_tensor("a_xxb", [128, 2 * 1024], BF16) as a_xxb, \
         nc.sbuf_tensor("a_mx", [128, NST], F32) as a_mx, \
         nc.sbuf_tensor("a_mn", [128, NST], F32) as a_mn:
        ln_cons = {}
        sp_done = {}
        for st in range(NST):
            bx = (st % 2) * 1024
            xs = a_x[:, bx:bx + 1024]
            tl = dma_in(xs, x_tm[st * 128:(st + 1) * 128, :],
                        waits=[ln_cons.get(st - 2)])
            tl = chase()
            if st % (T // 128) == 0:
                p.op(DV, "memset", a_pc[:, :], 0.0,
                     waits=[sp_done.get(st - 1)])
            t1 = p.op(DV, "reduce_sum", a_st[:, 0:1], xs, axis=AX.X,
                      waits=[tl] + tcons)
            t1 = p.op(DV, "scalar_tensor_tensor", a_sq[:, :], xs, 1.0, xs,
                      Op.bypass, Op.mult, accum_out=a_st[:, 1:2])
            p.op(DV, "tensor_scalar", a_st[:, 2:3], a_st[:, 0:1], 1.0 / 1024.0,
                 None, Op.mult)
            p.op(DV, "tensor_scalar", a_st[:, 3:4], a_st[:, 1:2], 1.0 / 1024.0,
                 LN_EPS, Op.mult, Op.add, selfw=[t1])
            p.op(DV, "tensor_mul", a_st[:, 4:5], a_st[:, 2:3], a_st[:, 2:3])
            p.op(DV, "tensor_sub", a_st[:, 3:4], a_st[:, 3:4], a_st[:, 4:5])
            heron_rstd(a_st, 3, 5, 6, 7, wide=False, iters=5)
            tnrm = p.op(DV, "tensor_scalar", a_sq[:, :], xs, a_st[:, 2:3],
                        a_st[:, 5:6], Op.subtract, Op.mult)
            ln_cons[st] = tnrm
            tev = None
            for ci in range(NC):
                bk = "b%d" % (ci % 2)
                slot = pf[ci % 2][:, (ci // 2 % 2) * 128:(ci // 2 % 2) * 128 + 128]
                tt = p.op(PE, "transpose", slot, a_sq[:, ci * 128:(ci + 1) * 128],
                          c_idf[:, :], waits=[tnrm, bwait(bk)])
                tev = p.op(AC, "activation", a_xnf[:, ci * 128:(ci + 1) * 128], slot,
                           AF.Identity, waits=[tt])
                bset(bk, tev)
            bo = (st % 2) * 1024
            xnf3 = chunked(a_xnf[:, :], 128)
            xxb3 = chunked(a_xxb[:, bo:bo + 1024], 128)
            t2 = p.op(DV, "tensor_sub", xxb3[:, :, 1:128],
                      xnf3[:, :, 0:127], xnf3[:, :, 1:128],
                      waits=[tev, sp_done.get(st - 2)])
            t2 = p.op(DV, "tensor_sub", xxb3[:, :, 0:1],
                      chunked(a_pc[:, :], 1), xnf3[:, :, 0:1])
            p.op(DV, "tensor_copy", chunked(a_pc[:, :], 1), xnf3[:, :, 127:128])
            t3 = p.op(DV, "tensor_scalar", a_xnb[:, bo:bo + 1024], a_xnf[:, :],
                      1.0, None, Op.mult, Op.max, accum_out=a_mx[:, st:st + 1])
            p.op(DV, "tensor_scalar", a_sq[:, :], a_xnf[:, :], 1.0,
                 None, Op.mult, Op.min, accum_out=a_mn[:, st:st + 1])
            d1 = dma_out(fm3(sp_xn, st * 128, 128),
                         chunked(a_xnb[:, bo:bo + 1024], 128), waits=[t3])
            d2 = dma_out(fm3(sp_xx, st * 128, 128),
                         chunked(a_xxb[:, bo:bo + 1024], 128), waits=[t2])
            sp_done[st] = gp_chase()
        tsxn = fin_scale(a_mx[:, 0:NST], a_mn[:, 0:NST], a_st, 0, 1)
        tsxn = p.op(DV, "tensor_scalar", scl[:, 12:13], scl[:, 0:1],
                    -CMAGIC, None, Op.mult, waits=[tsxn])
        if dbg:
            p.op(GP, "dma_start", out=dbg_sc, in_=scl[:, :], waits=[tsxn], dma=True)

    p.fence()

    if upto == "A":
        return _finish(nc, p, pes, es)

    pp_es = ExitStack()
    p_xn = pp_es.enter_context(nc.sbuf_tensor("p_xn", [128, 2 * CPT * 1024], BF16))
    p_xx = pp_es.enter_context(nc.sbuf_tensor("p_xx", [128, 2 * CPT * 1024], BF16))
    p_t = pp_es.enter_context(nc.sbuf_tensor("p_t", [128, CPT * 1024], BF16))
    p_mix = pp_es.enter_context(nc.sbuf_tensor("p_mix", [128, CPT * 1024], BF16))
    p_t5a = pp_es.enter_context(nc.sbuf_tensor("p_t5a", [128, TILE], BF16))
    p_t5c = pp_es.enter_context(nc.sbuf_tensor("p_t5c", [64, TILE], BF16))
    p_tdt = pp_es.enter_context(nc.sbuf_tensor("p_tdt", [64, TILE], BF16))
    p_r = pp_es.enter_context(nc.sbuf_tensor("p_r", [128, CPT * 1024], BF16))
    p_k = pp_es.enter_context(nc.sbuf_tensor("p_k", [128, CPT * 1024], BF16))
    p_v = pp_es.enter_context(nc.sbuf_tensor("p_v", [128, CPT * 1024], BF16))
    p_g = pp_es.enter_context(nc.sbuf_tensor("p_g", [128, CPT * 1024], BF16))
    p_th = pp_es.enter_context(nc.sbuf_tensor("p_th", [128, CPT * 1024], BF16))
    p_ww = pp_es.enter_context(nc.sbuf_tensor("p_ww", [128, CPT * 1024], BF16))
    p_c = pp_es.enter_context(nc.sbuf_tensor("p_c", [128, 1024], F32))
    p_s1 = pp_es.enter_context(nc.sbuf_tensor("p_s1", [128, 1024], F32))
    p_s2 = pp_es.enter_context(nc.sbuf_tensor("p_s2", [128, 2 * 1024], F32))
    p_el = pp_es.enter_context(nc.sbuf_tensor("p_el", [128, 16], F32))
    p_rt = pp_es.enter_context(nc.sbuf_tensor("p_rt", [128, 2 * 1024], BF16))
    p_kt = pp_es.enter_context(nc.sbuf_tensor("p_kt", [128, 2 * 1024], BF16))
    p_khtm = pp_es.enter_context(nc.sbuf_tensor("p_khtm", [128, 1024], BF16))
    p_pt = pp_es.enter_context(nc.sbuf_tensor("p_pt", [128, H * 128], BF16))
    p_y = pp_es.enter_context(nc.sbuf_tensor("p_y", [128, 1024], F32))
    p_yn = pp_es.enter_context(nc.sbuf_tensor("p_yn", [128, 1024], BF16))
    p_pd = pp_es.enter_context(nc.sbuf_tensor("p_pd", [128, 16], F32))
    p_gst = pp_es.enter_context(nc.sbuf_tensor("p_gst", [128, 64], F32))
    p_sb = pp_es.enter_context(nc.sbuf_tensor("p_sb", [128, NC * 128], BF16))
    p_z = pp_es.enter_context(nc.sbuf_tensor("p_z", [128, CPT * 1024], BF16))
    p_wst = pp_es.enter_context(nc.sbuf_tensor("p_w", [128, 2 * NC * 1024], BF16))
    p_xr = pp_es.enter_context(nc.sbuf_tensor("p_xr", [128, 512], F32))
    p_os = pp_es.enter_context(nc.sbuf_tensor("p_os", [128, 512], F32))
    if True:

        WIDX = {"r": 0, "k": 1, "v": 2, "g": 3, "o": 4}
        wslot = {}
        wld = {}

        def load_w(name, slot, waits=()):
            wslot[name] = slot
            dma_in(
                chunked(p_wst[:, slot * NC * 1024:(slot + 1) * NC * 1024], 1024),
                sp_w5[WIDX[name] * C:(WIDX[name] + 1) * C, :].rearrange(
                    "(a p) n -> p a n", p=128),
                waits=list(waits))
            wld[name] = chase()

        def wap(name, kc, oc):
            s0 = wslot[name] * NC * 1024 + kc * 1024 + oc * 128
            return p_wst[:, s0:s0 + 128]

        evh = [None, None]
        mixdone = {}
        WKV = {}
        for tl_i in range(NTIL):
            t0 = tl_i * TILE
            sl2 = (tl_i % 2) * CPT * 1024
            xns = p_xn[:, sl2:sl2 + CPT * 1024]
            xxs = p_xx[:, sl2:sl2 + CPT * 1024]
            tld1 = dma_in(chunked(xns, TILE), fm3(sp_xn, t0, TILE),
                          waits=[mixdone.get(tl_i - 2)] if tl_i >= 2
                          else [sp_done[NST - 1], p.tok(DV), p.tok(GP), p.tok(PE)])
            tld2 = dma_in(chunked(xxs, TILE), fm3(sp_xx, t0, TILE))
            tld2 = chase()
            if tl_i % TPS == 0:
                p.op(DV, "memset", SS[:, :], 0.0)
            txxx = None
            for ci in range(NC):
                txxx = p.op(DV, "scalar_tensor_tensor",
                            p_t[:, ci * TILE:(ci + 1) * TILE],
                            xxs[:, ci * TILE:(ci + 1) * TILE],
                            cs[:, ci * 8 + 0:ci * 8 + 1],
                            xns[:, ci * TILE:(ci + 1) * TILE],
                            Op.mult, Op.add, waits=[tld1, tld2])
            tm = tm2 = None
            for kc in range(NC):
                tm = p.op(PE, "matmul", pf[0][0:96, :], mw1[:, kc * 160:kc * 160 + 96],
                          p_t[:, kc * TILE:(kc + 1) * TILE],
                          start=(kc == 0), stop=(kc == NC - 1),
                          waits=[txxx, bwait("b0")])
                tm2 = p.op(PE, "matmul", pf[1][0:64, :],
                           mw1[:, kc * 160 + 96:kc * 160 + 160],
                           p_t[:, kc * TILE:(kc + 1) * TILE],
                           start=(kc == 0), stop=(kc == NC - 1),
                           waits=[bwait("b1")])
            tta = p.op(AC, "activation", p_t5a[0:96, :], pf[0][0:96, :], AF.Tanh,
                       waits=[tm, tm2])
            ttc = p.op(AC, "activation", p_t5c[0:64, :], pf[1][0:64, :], AF.Tanh)
            bset("b0", ttc)
            bset("b1", ttc)
            if upto == "P_maa" and tl_i == 0:
                return _finish(nc, p)

            def emit_mix(f, vcol, waits0):
                tev = None
                for ci in range(NC):
                    if f < 3:
                        rhs = p_t5a[f * 32:(f + 1) * 32, :]
                    else:
                        rhs = p_t5c[(f - 3) * 32:(f - 2) * 32, :]
                    bk = "b%d" % (2 + ci % 2)
                    fb = (f % 3) * 32
                    tmm = p.op(PE, "matmul", pf[2 + ci % 2][:, :],
                               mw2[fb:fb + 32, f * C + ci * 128:f * C + (ci + 1) * 128],
                               rhs, start=True, stop=True,
                               waits=list(waits0) + [bwait(bk)])
                    tev = p.op(AC, "activation", p_t[:, ci * TILE:(ci + 1) * TILE],
                               pf[2 + ci % 2][:, :], AF.Identity,
                               bias=cs[:, ci * 8 + vcol:ci * 8 + vcol + 1],
                               waits=[tmm])
                    bset(bk, tev)
                tq = p.op(DV, "tensor_mul", p_t[:, :], p_t[:, :], xxs,
                          waits=[tev])
                tmix = p.op(DV, "tensor_add", p_mix[:, :], p_t[:, :], xns)
                return tmix

            def emit_proj(dst, wname, waits0, act_g=False):
                tev = None
                for oc in range(NC):
                    bk = "b%d" % (4 + oc % 2)
                    bank = pf[4 + oc % 2]
                    tmm = None
                    for kc in range(NC):
                        tmm = p.op(PE, "matmul", bank[:, :], wap(wname, kc, oc),
                                   p_mix[:, kc * TILE:(kc + 1) * TILE],
                                   start=(kc == 0), stop=(kc == NC - 1),
                                   waits=(list(waits0) + [wld[wname], bwait(bk)])
                                   if kc == 0 else ())
                    if act_g:
                        tev = p.op(AC, "activation", p_g[:, oc * TILE:(oc + 1) * TILE],
                                   bank[:, :], AF.Identity, waits=[tmm])
                        tev = p.op(AC, "activation", p_th[:, oc * TILE:(oc + 1) * TILE],
                                   bank[:, :], AF.Tanh, scale=0.5)
                    else:
                        tev = p.op(DV, "tensor_copy", dst[:, oc * TILE:(oc + 1) * TILE],
                                   bank[:, :], waits=[tmm])
                    bset(bk, tev)
                return tev

            tmw = emit_mix(0, 1, [tta, ttc])
            tdm = None
            for kc in range(NC):
                tdm = p.op(PE, "matmul", pf[3][0:64, :], tw1[:, kc * 64:(kc + 1) * 64],
                           p_mix[:, kc * TILE:(kc + 1) * TILE],
                           start=(kc == 0), stop=(kc == NC - 1),
                           waits=[tmw, bwait("b3")] if kc == 0 else ())
            ttd = p.op(AC, "activation", p_tdt[0:64, :], pf[3][0:64, :], AF.Tanh,
                       waits=[tdm])
            bset("b3", ttd)
            tww = None
            for oc in range(NC):
                twm = p.op(PE, "matmul", pf[2][:, :],
                           tw2[0:64, oc * 128:(oc + 1) * 128], p_tdt[0:64, :],
                           start=True, stop=True, waits=[ttd, bwait("b2")])
                tww = p.op(AC, "activation", p_ww[:, oc * TILE:(oc + 1) * TILE],
                           pf[2][:, :], AF.Exp, bias=cs[:, oc * 8 + 6:oc * 8 + 7],
                           waits=[twm])
                bset("b2", tww)
            if upto == "P_w" and tl_i == 0:
                return _finish(nc, p)

            tmk = emit_mix(1, 2, [tmw])
            load_w("k", 0, waits=[p.tok(PE)] if tl_i else ())
            tpk = emit_proj(p_k, "k", [tmk])
            tmr = emit_mix(3, 4, [tmk])
            if upto == "P_kmix" and tl_i == 0:
                return _finish(nc, p)
            load_w("r", 1, waits=())
            tpr = emit_proj(p_r, "r", [tmr])
            if upto == "P_kr" and tl_i == 0:
                return _finish(nc, p)
            tmv = emit_mix(2, 3, [tmr])
            load_w("v", 0, waits=[tpk, p.tok(PE)])
            tpv = None
            for ts_ in range(CPT):
                for oh in range(2):
                    bk = "b%d" % (4 + (ts_ * 2 + oh) % 2)
                    bank = pf[4 + (ts_ * 2 + oh) % 2]
                    tmm = None
                    for kc in range(NC):
                        s0 = 0 * NC * 1024 + kc * 1024 + oh * 512
                        tmm = p.op(PE, "matmul", bank[:, :],
                                   p_mix[:, kc * TILE + ts_ * 128:kc * TILE + ts_ * 128 + 128],
                                   p_wst[:, s0:s0 + 512],
                                   start=(kc == 0), stop=(kc == NC - 1),
                                   waits=[tmv, wld["v"], bwait(bk)] if kc == 0 else ())
                    tpv = p.op(DV, "tensor_copy",
                               p_v[:, ts_ * 1024 + oh * 512:ts_ * 1024 + oh * 512 + 512],
                               bank[:, :], waits=[tmm])
                    bset(bk, tpv)
            tmg = emit_mix(4, 5, [tmv])
            mixdone[tl_i] = tmg
            load_w("g", 1, waits=[tpr, p.tok(PE)])
            tpg = emit_proj(None, "g", [tmg], act_g=True)
            load_w("o", 0, waits=[tpg, p.tok(PE)])
            tgf = p.op(DV, "scalar_tensor_tensor", p_g[:, :], p_th[:, :], 1.0,
                       p_g[:, :], Op.add, Op.mult, waits=[tpg])
            if upto == "P_g" and tl_i == 0:
                return _finish(nc, p)

            p_sq = p_t[:, 0:2048].bitcast(F32)
            quads = [[0, 2, 4, 6], [8, 10, 12, 14], [1, 3, 5, 7], [9, 11, 13, 15]]

            def wkv_head(j):
                jj = tl_i * CPT + j
                cs0 = j * 128
                sl = jj % 2
                rsl_r = chunked(p_r[:, :], TILE)[:, :, cs0:cs0 + 128]
                rsl_k = chunked(p_k[:, :], TILE)[:, :, cs0:cs0 + 128]
                rsl_w = chunked(p_ww[:, :], TILE)[:, :, cs0:cs0 + 128]
                s2 = p_s2[:, sl * 1024:sl * 1024 + 1024]
                rt = p_rt[:, sl * 1024:sl * 1024 + 1024]
                kt = p_kt[:, sl * 1024:sl * 1024 + 1024]
                el = p_el[:, sl * 8:sl * 8 + 8]
                tsc = None
                for ci in range(NC):
                    tsc = p.op(DV, "tensor_tensor_scan",
                               p_c[:, ci * 128:(ci + 1) * 128],
                               p_ww[:, ci * TILE + cs0:ci * TILE + cs0 + 128],
                               p_ww[:, ci * TILE + cs0:ci * TILE + cs0 + 128],
                               0.0, Op.add, Op.bypass,
                               waits=[tww, WKV.get(("te2", jj - 1))]
                               if ci == 0 else ())
                tel = p.op(AC, "activation", el,
                           p_c[:, 127::128].rearrange("p (a o) -> p a o", o=1),
                           AF.Exp, scale=-1.0,
                           waits=[tsc, WKV.get(("tsu", jj - 2))])
                tcx = p.op(DV, "tensor_sub", chunked(p_s1[:, :], 128),
                           chunked(p_c[:, :], 128), rsl_w)
                te1 = p.op(AC, "activation", p_s1[:, :], p_s1[:, :], AF.Exp,
                           scale=-1.0, waits=[tcx, WKV.get(("trt", jj - 1))])
                te2 = p.op(AC, "activation", s2, p_c[:, :], AF.Exp,
                           waits=[WKV.get(("tpd", jj - 2))])
                trt = p.op(DV, "tensor_mul", chunked(rt, 128), rsl_r,
                           chunked(p_s1[:, :], 128),
                           waits=[te1, WKV.get(("ty15", jj - 2))])
                tkt = p.op(DV, "tensor_mul", chunked(kt, 128), rsl_k,
                           chunked(s2, 128),
                           waits=[te2, WKV.get(("tkb", jj - 2))])
                truk = None
                for ci in range(NC):
                    truk = p.op(DV, "scalar_tensor_tensor",
                                s2[:, ci * 128:(ci + 1) * 128],
                                rsl_r[:, ci, :], cs[:, ci * 8 + 7:ci * 8 + 8],
                                rsl_k[:, ci, :], Op.mult, Op.mult)
                WKV[("te2", jj)] = te2
                WKV[("tel", jj)] = tel
                WKV[("trt", jj)] = trt
                WKV[("tkt", jj)] = tkt
                WKV[("truk", jj)] = truk

            wkv_head(0)
            for j in range(CPT):
                jj = tl_i * CPT + j
                cs0 = j * 128
                sl = jj % 2
                rsl_g = chunked(p_g[:, :], TILE)[:, :, cs0:cs0 + 128]
                s2 = p_s2[:, sl * 1024:sl * 1024 + 1024]
                rt = p_rt[:, sl * 1024:sl * 1024 + 1024]
                kt = p_kt[:, sl * 1024:sl * 1024 + 1024]
                el = p_el[:, sl * 8:sl * 8 + 8]
                tpd = None
                for hh in [h2 * 2 for h2 in range(8)] + [h2 * 2 + 1 for h2 in range(8)]:
                    ci, hb = hh // 2, (hh % 2) * 64
                    tpd = p.op(PE, "matmul", pf[3][:, hh:hh + 1],
                               s2[hb:hb + 64, ci * 128:(ci + 1) * 128],
                               c_ones[hb:hb + 64, 0:1], start=True, stop=True,
                               waits=[WKV[("truk", jj)], bwait("b3")]
                               if hh == 0 else ())
                WKV[("tpd", jj)] = tpd
                tpdc = p.op(DV, "tensor_copy", p_pd[:, :], pf[3][:, 0:16], waits=[tpd])
                bset("b3", tpdc)
                tmaskd = {}
                for gi, quad in enumerate(quads):
                    bk = "b%d" % (gi % 2)
                    bank = pf[gi % 2]
                    tg4 = None
                    for hq, hh in enumerate(quad):
                        ci, hb = hh // 2, (hh % 2) * 64
                        tg4 = p.op(PE, "matmul", bank[:, hq * 128:(hq + 1) * 128],
                                   kt[hb:hb + 64, ci * 128:(ci + 1) * 128],
                                   rt[hb:hb + 64, ci * 128:(ci + 1) * 128],
                                   start=True, stop=True,
                                   waits=[WKV[("trt", jj)], WKV[("tkt", jj)],
                                          bwait(bk)] if hq == 0 else ())
                    ptdst = p_pt[:, quad[0] * 128:quad[0] * 128 + 896]
                    ptdst = ptdst.rearrange("p (a k) -> p a k", k=128)[:, 0::2, :]
                    tmk_ = p.op(DV, "tensor_mul", ptdst,
                                bank[:, :].rearrange("p (a k) -> p a k", k=128),
                                c_mask[:, :].rearrange("p (a k) -> p a k", k=128),
                                waits=[tg4])
                    for hh in quad:
                        tmaskd[hh] = tmk_
                    bset(bk, tmk_)
                tlastpt = tg4
                tkh = None
                for ci in range(NC):
                    tkh = p.op(DV, "tensor_scalar", kt[:, ci * 128:(ci + 1) * 128],
                               kt[:, ci * 128:(ci + 1) * 128],
                               el[:, ci:ci + 1], None, Op.mult,
                               waits=[tlastpt, WKV[("tel", jj)]] if ci == 0 else ())
                tkb = None
                for ci in range(NC):
                    tkb = p.op(PE, "transpose", pb16[0][:, ci * 128:(ci + 1) * 128],
                               kt[:, ci * 128:(ci + 1) * 128], c_idb[:, :],
                               waits=[tkh, bwait("pb0")] if ci == 0 else ())
                WKV[("tkb", jj)] = tkb
                tktm = p.op(DV, "tensor_copy", p_khtm[:, :], pb16[0][:, :], waits=[tkb])
                bset("pb0", tktm)
                tsb = p.op(DV, "tensor_copy", p_sb[:, :], SS[:, :])
                if j + 1 < CPT:
                    wkv_head(j + 1)
                tylast = None
                for ci in range(NC):
                    bk = "b%d" % (2 + ci // 4)
                    bank = pf[2 + ci // 4]
                    slc = (ci % 4) * 128
                    p.op(PE, "matmul", bank[:, slc:slc + 128],
                         rt[:, ci * 128:(ci + 1) * 128],
                         p_sb[:, ci * 128:(ci + 1) * 128],
                         start=True, stop=False,
                         waits=[tmaskd[2 * ci], tmaskd[2 * ci + 1], tsb, bwait(bk)])
                    for hh in (2 * ci, 2 * ci + 1):
                        tylast = p.op(PE, "matmul",
                                      bank[:, slc + (hh % 2) * 64:slc + (hh % 2) * 64 + 64],
                                      p_pt[:, hh * 128:(hh + 1) * 128],
                                      p_v[:, j * 1024 + hh * 64:j * 1024 + hh * 64 + 64],
                                      start=False, stop=(hh % 2 == 1))
                WKV[("ty15", jj)] = tylast
                tyl = None
                for hh in range(H):
                    bank = pf[2 + hh // 8]
                    tyl = p.op(DV, "scalar_tensor_tensor",
                               p_y[:, hh * 64:(hh + 1) * 64],
                               p_v[:, j * 1024 + hh * 64:j * 1024 + hh * 64 + 64],
                               p_pd[:, hh:hh + 1],
                               bank[:, (hh % 8) * 64:(hh % 8) * 64 + 64],
                               Op.mult, Op.add,
                               waits=[tylast, tpdc])
                    bset("b%d" % (2 + hh // 8), tyl)
                tsu = None
                for ci in range(NC):
                    bk = "b%d" % (4 + ci % 2)
                    bank = pf[4 + ci % 2]
                    tu_ = p.op(PE, "matmul", bank[:, 0:128],
                               p_khtm[:, ci * 128:(ci + 1) * 128],
                               p_v[:, j * 1024 + ci * 128:j * 1024 + ci * 128 + 128],
                               start=True, stop=True,
                               waits=[tktm, bwait(bk)])
                    for hb in (0, 64):
                        tsu = p.op(DV, "scalar_tensor_tensor",
                                   SS[hb:hb + 64, ci * 128 + hb:ci * 128 + hb + 64],
                                   SS[hb:hb + 64, ci * 128 + hb:ci * 128 + hb + 64],
                                   el[hb:hb + 64, ci:ci + 1],
                                   bank[hb:hb + 64, hb:hb + 64],
                                   Op.mult, Op.add,
                                   waits=[tu_, tsb] if hb == 0 else ())
                    bset(bk, tsu)
                WKV[("tsu", jj)] = tsu
                p.op(DV, "reduce_sum", p_gst[:, 0:16],
                     p_y[:, :].rearrange("p (w k) -> p w k", k=64), axis=AX.X,
                     waits=[tyl])
                tsq = p.op(AC, "activation", p_sq[:, :], p_y[:, :], AF.Square,
                           waits=[tyl, WKV.get(("zdone", jj - 1))])
                p.op(DV, "reduce_sum", p_gst[:, 16:32],
                     p_sq[:, :].rearrange("p (w k) -> p w k", k=64), axis=AX.X,
                     waits=[tsq])
                p.op(DV, "tensor_scalar", p_gst[:, 0:16], p_gst[:, 0:16],
                     1.0 / 64.0, None, Op.mult)
                p.op(DV, "tensor_scalar", p_gst[:, 16:32], p_gst[:, 16:32],
                     1.0 / 64.0, GN_EPS, Op.mult, Op.add)
                p.op(DV, "tensor_mul", p_gst[:, 32:48], p_gst[:, 0:16], p_gst[:, 0:16])
                p.op(DV, "tensor_sub", p_gst[:, 16:32], p_gst[:, 16:32],
                     p_gst[:, 32:48])
                gsl = lambda c0: p_gst[:, c0:c0 + 16]
                wv, sv, tv = gsl(16), gsl(32), gsl(48)
                p.op(DV, "tensor_scalar", sv, wv, 0.1, 0.95, Op.is_ge, Op.mult)
                p.op(DV, "tensor_scalar", sv, sv, 0.05, None, Op.add)
                p.op(DV, "tensor_scalar", tv, wv, 10.0, 29.0, Op.is_ge, Op.mult)
                p.op(DV, "tensor_add", sv, sv, tv)
                p.op(DV, "tensor_scalar", tv, wv, 1000.0, 670.0, Op.is_ge, Op.mult)
                p.op(DV, "tensor_add", sv, sv, tv)
                for _ in range(6):
                    p.op(DV, "reciprocal", tv, sv)
                    p.op(DV, "tensor_mul", tv, tv, wv)
                    p.op(DV, "tensor_add", tv, tv, sv)
                    p.op(DV, "tensor_scalar", sv, tv, 0.5, None, Op.mult)
                p.op(DV, "reciprocal", tv, sv)
                p.op(DV, "tensor_mul", sv, tv, tv)
                p.op(DV, "tensor_mul", sv, sv, wv)
                p.op(DV, "tensor_scalar", sv, sv, -0.5, 1.5, Op.mult, Op.add)
                p.op(DV, "tensor_mul", sv, tv, sv)
                tgn = None
                for hh in range(H):
                    tgn = p.op(DV, "tensor_scalar", p_yn[:, hh * 64:(hh + 1) * 64],
                               p_y[:, hh * 64:(hh + 1) * 64],
                               p_gst[:, hh:hh + 1], p_gst[:, 32 + hh:33 + hh],
                               Op.subtract, Op.mult)
                tz = p.op(DV, "tensor_mul", p_sq[:, :], p_yn[:, :], c_lnw[:, :],
                          waits=[tgn, tsq])
                tz = p.op(DV, "tensor_add", p_sq[:, :], p_sq[:, :], c_lnb[:, :])
                tz = p.op(DV, "tensor_copy", p_yn[:, :], p_sq[:, :])
                WKV[("zdone", jj)] = tz
                tzt = None
                for ci in range(NC):
                    tzt = p.op(PE, "transpose", pb16[1][:, ci * 128:(ci + 1) * 128],
                               p_yn[:, ci * 128:(ci + 1) * 128], c_idb[:, :],
                               waits=[tz, bwait("pb1")] if ci == 0 else ())
                prev_z = p.op(DV, "tensor_mul",
                              chunked(p_z[:, :], TILE)[:, :, cs0:cs0 + 128],
                              chunked(pb16[1][:, :], 128), rsl_g,
                              waits=[tzt, tgf])
                bset("pb1", prev_z)
            if upto == "P_z" and tl_i == 0:
                return _finish(nc, p)
            for oc in (reversed(range(NC)) if dbgout == "ocrev" else range(NC)):
                if dbgout == "mmb2":
                    bk = "b%d" % (2 + oc % 2)
                    bank = pf[2 + oc % 2]
                else:
                    bk = "b%d" % (4 + oc % 2)
                    bank = pf[4 + oc % 2]
                tmo = None
                for kc in range(NC):
                    tmo = p.op(PE, "matmul", bank[:, :], wap("o", kc, oc),
                               p_z[:, kc * TILE:(kc + 1) * TILE],
                               start=(kc == 0), stop=(kc == NC - 1),
                               waits=[prev_z, wld["o"], bwait(bk)] if kc == 0 else ())
                xr = p_xr[:, :]
                txl = dma_in(xr, x_fm[oc * 128:(oc + 1) * 128, t0:t0 + TILE],
                             waits=[evh[0]])
                txl = chase()
                tout = p.op(DV, "tensor_add", p_os[:, :], bank[:, :], xr,
                            waits=[tmo, txl, bwait("pos")])
                evh[0] = tout
                bset(bk, tout)
                bset("pos", dma_out(out_fm[oc * 128:(oc + 1) * 128, t0:t0 + TILE],
                                    p_os[:, :], waits=[tout]))

    pp_es.close()
    p.fence()

    if upto == "P":
        return _finish(nc, p, pes, es)

    with nc.sbuf_tensor("m_wu", [128, NC * I3], BF16) as m_wu, \
         nc.sbuf_tensor("m_wg", [128, NC * I3], BF16) as m_wg, \
         nc.sbuf_tensor("m_xn", [128, 2 * NC * 512], BF16) as m_xn, \
         nc.sbuf_tensor("m_tq", [128, NC * 512], F32) as m_tq, \
         nc.sbuf_tensor("m_xq", [128, 2 * NC * 512], BF16) as m_xq, \
         nc.sbuf_tensor("m_ob", [128, 8 * 512], BF16) as m_ob, \
         nc.sbuf_tensor("m_mx", [128, 2 * NI * 16], F32) as m_mx, \
         nc.sbuf_tensor("m_mn", [128, 2 * NI * 16], F32) as m_mn, \
         nc.sbuf_tensor("m_sc", [128, 512], F32) as m_sc, \
         nc.sbuf_tensor("m_st", [128, 16], F32) as m_st:
        p.op(DV, "memset", m_mx[:, :], 0.0)
        p.op(DV, "memset", m_mn[:, :], 0.0)
        twu = dma_in(chunked(m_wu[:, :], I3), sp_wupT.rearrange("(a p) n -> p a n", p=128))
        twg = dma_in(chunked(m_wg[:, :], I3), sp_wgateT.rearrange("(a p) n -> p a n", p=128))
        twu = twg = chase()
        slot_done = {}
        q1h, mmh = {}, {}
        for tl_i in range(NTIL):
            t0 = tl_i * TILE
            sl = (tl_i % 2) * NC * 512
            tld = dma_in(chunked(m_xn[:, sl:sl + NC * 512], 512), fm3(sp_xn, t0, TILE),
                         waits=[q1h.get(tl_i - 2)] if tl_i >= 2
                         else [p.tok(DV), p.tok(AC), p.tok(GP)])
            tld = chase()
            tq1 = p.op(AC, "activation", m_tq[:, :], m_xn[:, sl:sl + NC * 512],
                       AF.Identity, bias=c_cc[:, 0:1], scale=scl[:, 1:2],
                       waits=[tld, tsxn])
            q1h[tl_i] = tq1
            tq2 = p.op(AC, "activation", m_xq[:, sl:sl + NC * 512], m_tq[:, :],
                       AF.Identity, bias=scl[:, 12:13], scale=scl[:, 0:1],
                       waits=[mmh.get(tl_i - 2)])
            for iw, (wsb, spo, mxo, tw) in enumerate(
                    ((m_wu, sp_up, 0, twu), (m_wg, sp_gate, 1, twg))):
                for oc in range(NI):
                    slot = iw * 4 + oc % 4
                    bk = "b%d" % (iw * 2 + oc % 2)
                    bank = pf[iw * 2 + oc % 2]
                    tmm = None
                    for kc in range(NC):
                        tmm = p.op(PE, "matmul", bank[:, :],
                                   wsb[:, kc * I3 + oc * 128:kc * I3 + oc * 128 + 128],
                                   m_xq[:, sl + kc * 512:sl + (kc + 1) * 512],
                                   start=(kc == 0), stop=(kc == NC - 1),
                                   waits=[tq2, tw, bwait(bk)] if kc == 0 else ())
                    mmh[tl_i] = tmm
                    mcol = mxo * NI * 16 + oc * 16 + tl_i
                    tev = p.op(DV, "tensor_scalar",
                               m_ob[:, slot * 512:(slot + 1) * 512], bank[:, :],
                               1.0, None, Op.mult, Op.max,
                               accum_out=m_mx[:, mcol:mcol + 1],
                               waits=[tmm, slot_done.get(slot)])
                    bset(bk, tev)
                    p.op(DV, "tensor_scalar", m_sc[:, :],
                         m_ob[:, slot * 512:(slot + 1) * 512], 1.0, None,
                         Op.mult, Op.min, accum_out=m_mn[:, mcol:mcol + 1])
                    dma_out(
                        spo[oc * 128:(oc + 1) * 128, t0:t0 + TILE],
                        m_ob[:, slot * 512:(slot + 1) * 512], waits=[tev])
                    if oc % 4 == 3:
                        tch4 = gp_chase()
                        for q_ in range(4):
                            slot_done[iw * 4 + q_] = tch4
        tlast = [slot_done[s] for s in slot_done]
        fin_scale(m_mx[:, 0:NI * 16], m_mn[:, 0:NI * 16], m_st, 2, 3, extra=[p.tok(DV)])
        tsg = fin_scale(m_mx[:, NI * 16:2 * NI * 16], m_mn[:, NI * 16:2 * NI * 16], m_st, 4, 5)
        tsg = p.op(DV, "tensor_scalar", scl[:, 10:11], scl[:, 4:5], 0.5, None,
                   Op.mult, waits=[tsg])
        tsg = p.op(DV, "tensor_scalar", scl[:, 13:14], scl[:, 10:11],
                   -CMAGIC, None, Op.mult, waits=[tsg])

    p.fence()

    if upto == "M":
        return _finish(nc, p, pes, es)

    I3C = NI * 128
    with nc.sbuf_tensor("e_g", [128, 2 * I3C], BF16) as e_g, \
         nc.sbuf_tensor("e_t1", [128, 2 * I3C], F32) as e_t1, \
         nc.sbuf_tensor("e_t2", [128, 2 * I3C], F32) as e_t2, \
         nc.sbuf_tensor("e_t3", [128, 2 * I3C], F32) as e_t3, \
         nc.sbuf_tensor("e_o", [128, 2 * I3C], BF16) as e_o, \
         nc.sbuf_tensor("e_mx", [128, NST], F32) as e_mx, \
         nc.sbuf_tensor("e_mn", [128, NST], F32) as e_mn, \
         nc.sbuf_tensor("e_st", [128, 16], F32) as e_st:
        p.op(DV, "memset", e_mn[:, :], 0.0)
        q1h, d3h, d4h, sch = {}, {}, {}, {}
        for et in range(NST):
            t0 = et * 128
            s0 = (et % 2) * I3C
            g_ = e_g[:, s0:s0 + I3C]
            t1 = e_t1[:, s0:s0 + I3C]
            t2 = e_t2[:, s0:s0 + I3C]
            t3 = e_t3[:, s0:s0 + I3C]
            o_ = e_o[:, s0:s0 + I3C]
            tld = dma_in(chunked(g_, 128), fm3(sp_gate, t0, 128),
                         waits=[q1h.get(et - 2)] if et >= 2
                         else [p.tok(DV), p.tok(AC), p.tok(GP)])
            tld = chase()
            q1h[et] = p.op(AC, "activation", t1, g_, AF.Identity,
                           bias=c_cc[:, 0:1], scale=scl[:, 5:6],
                           waits=[tld, tsg, d3h.get(et - 2)])
            p.op(AC, "activation", t2, t1, AF.Identity,
                 bias=scl[:, 13:14], scale=scl[:, 10:11],
                 waits=[d4h.get(et - 2)])
            p.op(AC, "activation", t3, t2, AF.Tanh, waits=[d4h.get(et - 2)])
            q4 = p.op(AC, "activation", t1, t3, AF.Identity,
                      bias=c_cc[:, 1:2], scale=32768.0)
            d3h[et] = p.op(DV, "tensor_scalar", t3, t1, c_cc[:, 0:1],
                           65535.0, Op.subtract, Op.min, waits=[q4])
            d4o = p.op(DV, "scalar_tensor_tensor", o_, t2,
                       2.0 / 65536.0, t3, Op.mult, Op.mult,
                       waits=[sch.get(et - 2)])
            p.op(DV, "tensor_scalar", t3, o_, 1.0, None,
                 Op.mult, Op.max, accum_out=e_mx[:, et:et + 1])
            d4h[et] = p.op(DV, "tensor_scalar", t3, o_, 1.0, None,
                           Op.mult, Op.min, accum_out=e_mn[:, et:et + 1])
            dma_out(fm3(sp_g2p, t0, 128), chunked(o_, 128), waits=[d4o])
            sch[et] = gp_chase()
        ts3 = fin_scale(e_mx[:, :], e_mn[:, :], e_st, 6, 7,
                        extra=[p.tok(DV), p.tok(GP)])
        ts3 = p.op(DV, "tensor_mul", scl[:, 11:12], scl[:, 2:3], scl[:, 6:7],
                   waits=[ts3])
        ts3 = p.op(DV, "tensor_scalar", scl[:, 14:15], scl[:, 11:12],
                   -CMAGIC, None, Op.mult, waits=[ts3])

    p.fence()

    if upto == "E1":
        return _finish(nc, p, pes, es)

    with nc.sbuf_tensor("f_g", [128, 2 * I3C], BF16) as f_g, \
         nc.sbuf_tensor("f_u", [128, 2 * I3C], BF16) as f_u, \
         nc.sbuf_tensor("f_t1", [128, 2 * I3C], F32) as f_t1, \
         nc.sbuf_tensor("f_t2", [128, 2 * I3C], F32) as f_t2, \
         nc.sbuf_tensor("f_o", [128, 2 * I3C], BF16) as f_o, \
         nc.sbuf_tensor("f_mx", [128, NST], F32) as f_mx, \
         nc.sbuf_tensor("f_mn", [128, NST], F32) as f_mn, \
         nc.sbuf_tensor("f_st", [128, 16], F32) as f_st:
        p.op(DV, "memset", f_mn[:, :], 0.0)
        q1h, d2h, sch = {}, {}, {}
        for et in range(NST):
            t0 = et * 128
            s0 = (et % 2) * I3C
            g_ = f_g[:, s0:s0 + I3C]
            u_ = f_u[:, s0:s0 + I3C]
            t1 = f_t1[:, s0:s0 + I3C]
            t2 = f_t2[:, s0:s0 + I3C]
            o_ = f_o[:, s0:s0 + I3C]
            dma_in(chunked(g_, 128), fm3(sp_g2p, t0, 128),
                   waits=[q1h.get(et - 2)] if et >= 2
                   else [p.tok(DV), p.tok(AC), p.tok(GP)])
            dma_in(chunked(u_, 128), fm3(sp_up, t0, 128))
            tld = chase()
            p.op(AC, "activation", t1, g_, AF.Identity,
                 bias=c_cc[:, 0:1], scale=scl[:, 7:8],
                 waits=[tld, ts3, d2h.get(et - 2)])
            q1h[et] = p.op(AC, "activation", t2, u_, AF.Identity,
                           bias=c_cc[:, 0:1], scale=scl[:, 3:4],
                           waits=[d2h.get(et - 2)])
            q3 = p.op(AC, "activation", t2, t2, AF.Identity,
                      bias=scl[:, 14:15], scale=scl[:, 11:12])
            d2o = p.op(DV, "scalar_tensor_tensor", o_, t1,
                       c_cc[:, 0:1], t2, Op.subtract, Op.mult,
                       waits=[q3, sch.get(et - 2)])
            p.op(DV, "tensor_scalar", t1, o_, 1.0, None,
                 Op.mult, Op.max, accum_out=f_mx[:, et:et + 1])
            d2h[et] = p.op(DV, "tensor_scalar", t1, o_, 1.0, None,
                           Op.mult, Op.min, accum_out=f_mn[:, et:et + 1])
            dma_out(fm3(sp_opre, t0, 128), chunked(o_, 128), waits=[d2o])
            sch[et] = gp_chase()
        ts4 = fin_scale(f_mx[:, :], f_mn[:, :], f_st, 8, 9,
                        extra=[p.tok(DV), p.tok(GP)])
        ts4 = p.op(DV, "tensor_scalar", scl[:, 15:16], scl[:, 8:9],
                   -CMAGIC, None, Op.mult, waits=[ts4])

    p.fence()

    if upto == "E2":
        return _finish(nc, p, pes, es)

    _gp_after_P = p.cnt["gpd"]

    def _gpfence(h, v=_gp_after_P, s=p.sem["gpd"]):
        h.wait_ge(s, v)
    p.items[GP].append(_gpfence)

    with nc.sbuf_tensor("g_wd", [128, NI * C], BF16) as g_wd, \
         nc.sbuf_tensor("g_op", [128, 2 * NI * 512], BF16) as g_op, \
         nc.sbuf_tensor("g_tq", [128, 2 * 512], F32) as g_tq, \
         nc.sbuf_tensor("g_ob", [128, NI * 512], BF16) as g_ob, \
         nc.sbuf_tensor("g_fs", [128, 2 * 512], F32) as g_fs:
        twd = dma_in(chunked(g_wd[:, :], C), sp_wdownT.rearrange("(a p) n -> p a n", p=128))
        twd = chase()
        qh, mmic = {}, {}
        for tl_i in range(NTIL):
            t0 = tl_i * TILE
            s0 = (tl_i % 2) * NI * 512
            tld = dma_in(chunked(g_op[:, s0:s0 + NI * 512], 512),
                         fm3(sp_opre, t0, TILE),
                         waits=[qh.get(tl_i - 2)] if tl_i >= 2
                         else [p.tok(DV), p.tok(AC), p.tok(GP)])
            tld = chase()
            tqq = None
            for ic in range(NI):
                ta = p.op(AC, "activation", g_tq[:, (ic % 2) * 512:(ic % 2) * 512 + 512],
                          g_op[:, s0 + ic * 512:s0 + ic * 512 + 512], AF.Identity,
                          bias=c_cc[:, 0:1], scale=scl[:, 9:10],
                          waits=[tld, ts4] if ic == 0 else ())
                tqq = p.op(AC, "activation", g_ob[:, ic * 512:(ic + 1) * 512],
                           g_tq[:, (ic % 2) * 512:(ic % 2) * 512 + 512], AF.Identity,
                           bias=scl[:, 15:16], scale=scl[:, 8:9],
                           waits=[mmic.get((tl_i - 1, ic))])
            qh[tl_i] = ta
            for hc in range(NC):
                bk = "b%d" % (4 + hc % 2)
                bank = pf[4 + hc % 2]
                tmm = None
                for ic in range(NI):
                    tmm = p.op(PE, "matmul", bank[:, :],
                               g_wd[:, ic * C + hc * 128:ic * C + hc * 128 + 128],
                               g_ob[:, ic * 512:(ic + 1) * 512],
                               start=(ic == 0), stop=(ic == NI - 1),
                               waits=[tqq, twd, bwait(bk)] if ic == 0 else ())
                    if hc == NC - 1:
                        mmic[(tl_i, ic)] = tmm
                tev = p.op(DV, "tensor_copy", g_fs[:, (hc % 2) * 512:(hc % 2) * 512 + 512],
                           bank[:, :], waits=[tmm, bwait("gfs%d" % (hc % 2))])
                bset(bk, tev)
                dma_out(out_fm[hc * 128:(hc + 1) * 128, t0:t0 + TILE],
                        g_fs[:, (hc % 2) * 512:(hc % 2) * 512 + 512],
                        waits=[tev], accum=True)
                bset("gfs%d" % (hc % 2), gp_chase())

    fgp = p.cnt["gpd"]

    def fin(h, fgp=fgp, s=p.sem["gpd"]):
        h.wait_ge(s, fgp)
    p.items[GP].append(fin)

    with nc.Block() as block:
        p.finalize(block)
    pes.close()
    es.close()
    return nc


def _host_inputs(inputs, core, bpc=4, T=2048):
    TOK = bpc * T
    x = np.asarray(inputs["L0_attn_x"][core * bpc:(core + 1) * bpc, :T], np.float32)
    x2 = np.ascontiguousarray(x.reshape(TOK, C))
    x_fm = np.ascontiguousarray(x2.T)
    w5 = np.concatenate([np.asarray(inputs["L0_attn_W_r"], np.float32),
                         np.asarray(inputs["L0_attn_W_k"], np.float32),
                         np.asarray(inputs["L0_attn_W_v"], np.float32),
                         np.asarray(inputs["L0_attn_W_g"], np.float32),
                         np.asarray(inputs["L0_attn_W_o"], np.float32)], axis=0)
    maa2 = np.ascontiguousarray(np.tile(np.asarray(
        inputs["L0_attn_time_maa_w2"], np.float32).transpose(1, 0, 2).reshape(32, 5 * C),
        (4, 1)))
    vecs = np.stack([
        np.asarray(inputs["L0_attn_time_maa_x"], np.float32).reshape(C),
        np.asarray(inputs["L0_attn_time_maa_w"], np.float32).reshape(C),
        np.asarray(inputs["L0_attn_time_maa_k"], np.float32).reshape(C),
        np.asarray(inputs["L0_attn_time_maa_v"], np.float32).reshape(C),
        np.asarray(inputs["L0_attn_time_maa_r"], np.float32).reshape(C),
        np.asarray(inputs["L0_attn_time_maa_g"], np.float32).reshape(C),
        np.asarray(inputs["L0_attn_time_decay"], np.float32).reshape(C),
        np.asarray(inputs["L0_attn_time_faaaa"], np.float32).reshape(C)],
        axis=1).astype(np.float32)
    lnwb = np.concatenate([
        np.tile(np.asarray(inputs["L0_attn_ln_w"], np.float32).reshape(1, C), (128, 1)),
        np.tile(np.asarray(inputs["L0_attn_ln_b"], np.float32).reshape(1, C), (128, 1))],
        axis=1)
    bind_up = np.zeros((32, NC * 128), np.float32)
    for cc in range(NC):
        for pp in range(128):
            bind_up[cc * 4 + pp // 32, cc * 128 + pp] = 1.0
    bind_dn = np.zeros((96, NI * 128), np.float32)
    for cc in range(NI):
        for pp in range(128):
            bind_dn[cc * 4 + pp // 32, cc * 128 + pp] = 1.0
    return {
        "x_tm": x2, "x_fm": x_fm, "w5": w5,
        "maa_w1": np.asarray(inputs["L0_attn_time_maa_w1"], np.float32),
        "maa_w2": maa2,
        "td_w1": np.asarray(inputs["L0_attn_time_decay_w1"], np.float32),
        "td_w2": np.asarray(inputs["L0_attn_time_decay_w2"], np.float32),
        "vecs": vecs, "lnwb": lnwb.astype(ml_dtypes.bfloat16),
        "identf": np.eye(128, dtype=np.float32),
        "identb": np.eye(128, dtype=ml_dtypes.bfloat16),
        "mask4": np.tile(np.triu(np.ones((128, 128), np.float32), 1),
                         (1, 4)).astype(ml_dtypes.bfloat16),
        "onesf": np.ones((128, 128), np.float32),
        "bind_up": bind_up, "bind_dn": bind_dn,
        "wup_n": np.asarray(inputs["L0_ffn_w_up"], np.float32),
        "wgate_n": np.asarray(inputs["L0_ffn_w_gate"], np.float32),
        "wdown_n": np.asarray(inputs["L0_ffn_w_down"], np.float32),
        "wupT": np.ascontiguousarray(np.asarray(inputs["L0_ffn_w_up"], np.float32).T),
        "wgateT": np.ascontiguousarray(np.asarray(inputs["L0_ffn_w_gate"], np.float32).T),
        "wdownT": np.ascontiguousarray(np.asarray(inputs["L0_ffn_w_down"], np.float32).T),
        "cconst": np.tile(np.array([[CMAGIC, CMAGIC + 32768.0]], np.float32), (128, 1)),
    }


def kernel(**inputs):
    bpc = 32 // NCORES
    nc = build_program(bpc=bpc, T=2048)
    in_maps = [_host_inputs(inputs, c, bpc=bpc) for c in range(NCORES)]
    res = run_bass_kernel_spmd(nc, in_maps, core_ids=list(range(NCORES)))
    outs = []
    for c in range(NCORES):
        o = np.asarray(res.results[c]["out_fm"])
        outs.append(np.ascontiguousarray(o.T).reshape(bpc, 2048, C))
    return np.concatenate(outs, axis=0)


# version 16
# speedup vs baseline: 1.0193x; 1.0193x over previous
import numpy as np
import ml_dtypes
import concourse.bass as bass
import concourse.mybir as mybir
from concourse.bass_utils import run_bass_kernel_spmd
from concourse.alu_op_type import AluOpType as Op
from contextlib import ExitStack

F32 = mybir.dt.float32
BF16 = mybir.dt.bfloat16
I32 = mybir.dt.int32
AF = mybir.ActivationFunctionType
AX = mybir.AxisListType

NCORES = 8
C = 1024
NC = 8
H = 16
HD = 64
I3 = 3072
NI = 24
CMAGIC = 1.5 * 2.0 ** 23
QMAX = 32767.0
LN_EPS = 1e-5
GN_EPS = 1e-5 * 64

LD, GP, PE, DV, AC = "sync", "gp", "pe", "dve", "act"


class P:

    ENGS = (LD, GP, PE, DV, AC)
    SEMS = ("ld", "gpd", "gpc", "pe", "dve", "act")
    SEM_ENG = {"ld": LD, "gpd": GP, "gpc": GP, "pe": PE, "dve": DV, "act": AC}

    def __init__(self, nc):
        self.nc = nc
        self.items = {e: [] for e in self.ENGS}
        self.es = ExitStack()
        self.sem = {s: self.es.enter_context(nc.semaphore("s_" + s)) for s in self.SEMS}
        self.cnt = {s: 0 for s in self.SEMS}
        self.waited = {e: {} for e in self.ENGS}
        self.bank = {}
        self.fence_pend = {e: [] for e in self.ENGS}
        self.prev_small = {e: None for e in self.ENGS}
        self.prev_out = {e: None for e in self.ENGS}

    def _sem_of(self, e, dma):
        if e == GP:
            return "gpd" if dma else "gpc"
        if e == LD:
            return "ld"
        return {PE: "pe", DV: "dve", AC: "act"}[e]

    def tok(self, e):
        if e == GP:
            return [("gpd", self.cnt["gpd"]), ("gpc", self.cnt["gpc"])]
        return [(self._sem_of(e, e == LD), self.cnt[self._sem_of(e, e == LD)])]

    def _flat(self, waits):
        out = []
        for t in waits:
            if t is None:
                continue
            if isinstance(t, list):
                out.extend(x for x in t if x is not None)
            else:
                out.append(t)
        return out

    def _mkwaits(self, e, waits):
        best = {}
        for (sn, v) in self._flat(waits):
            if v <= 0 or self.SEM_ENG[sn] == e:
                continue
            if v > best.get(sn, -1):
                best[sn] = v
        wl = []
        wd = self.waited[e]
        for sn, v in best.items():
            if wd.get(sn, -1) >= v:
                continue
            wd[sn] = v
            wl.append((self.sem[sn], v))
        return wl

    def fence(self):
        snap = [(s, self.cnt[s]) for s in self.SEMS]
        for e in self.ENGS:
            self.fence_pend[e] = snap

    def op(self, e, meth, *args, waits=(), selfw=(), inc=True, dma=False, **kw):
        amt = 16 if dma else 1
        sn = self._sem_of(e, dma)
        if inc:
            self.cnt[sn] += amt
        waits = self._flat(waits)
        if self.fence_pend[e]:
            waits = waits + self.fence_pend[e]
            self.fence_pend[e] = []
        wl = self._mkwaits(e, waits)
        for (sn2, v2) in self._flat(selfw):
            if self.waited[e].get("#" + sn2, -1) < v2:
                self.waited[e]["#" + sn2] = v2
                wl.append((self.sem[sn2], v2))
        if e in (DV, AC) and self.prev_small[e] is not None:
            sn2, v2 = self.prev_small[e]
            if self.waited[e].get("#" + sn2, -1) < v2:
                self.waited[e]["#" + sn2] = v2
                wl.append((self.sem[sn2], v2))
        def _names(aps):
            s = set()
            for a_ in aps:
                try:
                    s.add(a_.tensor.name)
                except Exception:
                    pass
            return s

        cur_w = _names([args[0] if args else None, kw.get("accum_out")])
        cur_r = _names(list(args[1:]) + [kw.get("bias"), kw.get("scale")])
        if e in (DV, AC, GP) and not dma and self.prev_out.get(e) is not None:
            (sn2, v2), pw, pr = self.prev_out[e]
            if ((cur_r & pw) or (cur_w & (pw | pr))) and \
                    self.waited[e].get("#" + sn2, -1) < v2:
                self.waited[e]["#" + sn2] = v2
                wl.append((self.sem[sn2], v2))
        if e in (DV, AC, GP) and not dma and inc:
            self.prev_out[e] = ((sn, self.cnt[sn]), cur_w, cur_r)
        if e in (DV, AC) and inc:
            small = bool(kw.get("accum_out") is not None)
            if not small and meth != "engine_nop" and args:
                try:
                    a0 = args[0]
                    cnts = 1
                    for (st_, ct_) in list(a0.ap)[1:]:
                        cnts *= ct_
                    small = cnts * (2 if "bfloat" in str(a0.tensor.dtype) else 4) < 512
                except Exception:
                    small = True
            self.prev_small[e] = (sn, self.cnt[sn]) if small else None
        mysem = self.sem[sn]

        def fn(h, meth=meth, args=args, kw=kw, wl=wl, inc=inc, amt=amt):
            for (s, v) in wl:
                h.wait_ge(s, v)
            ins = getattr(h, meth)(*args, **kw)
            if inc:
                ins.then_inc(mysem, amt)
        self.items[e].append(fn)
        return (sn, self.cnt[sn])

    def waitonly(self, e, waits):
        wl = self._mkwaits(e, waits)
        if not wl:
            return

        def fn(h, wl=wl):
            for (s, v) in wl:
                h.wait_ge(s, v)
        self.items[e].append(fn)

    def finalize(self, block):
        def run(e):
            def _(h):
                for fn in self.items[e]:
                    fn(h)
            return _
        block.sync(run(LD))
        block.gpsimd(run(GP))
        block.tensor(run(PE))
        block.vector(run(DV))
        block.scalar(run(AC))


def _finish(nc, p, pes=None, es=None):
    fgp = p.cnt["gpd"]

    def fin(h, fgp=fgp, s=p.sem["gpd"]):
        h.wait_ge(s, fgp)
    p.items[GP].append(fin)
    with nc.Block() as block:
        p.finalize(block)
    if pes is not None:
        pes.close()
    if es is not None:
        es.close()
    return nc


def build_program(bpc=4, T=2048, dbg=False, upto=None, dbgout=None):
    TOK = bpc * T
    TILE = 512
    NTIL = TOK // TILE
    NST = TOK // 128
    TPS = T // TILE
    CPT = TILE // 128

    nc = bass.Bass("TRN2", target_bir_lowering=False, debug=False)

    def din(name, shape, dt=F32):
        return nc.dram_tensor(name, list(shape), dt, kind="ExternalInput").ap()

    x_tm = din("x_tm", (TOK, C))
    x_fm = din("x_fm", (C, TOK))
    w5 = din("w5", (5 * C, C))
    maa_w1 = din("maa_w1", (C, 160))
    maa_w2 = din("maa_w2", (128, 5 * C))
    td_w1 = din("td_w1", (C, 64))
    td_w2 = din("td_w2", (64, C))
    vecs = din("vecs", (C, 8))
    lnwb = din("lnwb", (128, 2 * C), BF16)
    identf = din("identf", (128, 128))
    identb = din("identb", (128, 128), BF16)
    mask4 = din("mask4", (128, 512), BF16)
    onesf = din("onesf", (128, 128))
    bind_up = din("bind_up", (32, NC * 128))
    bind_dn = din("bind_dn", (96, NI * 128))
    cconst = din("cconst", (128, 2))
    wup_n = din("wup_n", (I3, C))
    wgate_n = din("wgate_n", (I3, C))
    wdown_n = din("wdown_n", (C, I3))
    wupT = din("wupT", (C, I3))
    wgateT = din("wgateT", (C, I3))
    wdownT = din("wdownT", (I3, C))

    out_fm = nc.dram_tensor("out_fm", [C, TOK], F32, kind="ExternalOutput").ap()

    dk = "ExternalOutput" if dbg else "Internal"
    sp_xn = nc.dram_tensor("sp_xn", [C, TOK], BF16, kind=dk).ap()
    sp_xx = nc.dram_tensor("sp_xx", [C, TOK], BF16, kind=dk).ap()
    sp_w5 = nc.dram_tensor("sp_w5", [5 * C, C], BF16).ap()
    sp_wupT = nc.dram_tensor("sp_wupT", [C, I3], BF16, kind=dk).ap()
    sp_wgateT = nc.dram_tensor("sp_wgateT", [C, I3], BF16).ap()
    sp_wdownT = nc.dram_tensor("sp_wdownT", [I3, C], BF16).ap()
    sp_up = nc.dram_tensor("sp_up", [I3, TOK], BF16, kind=dk).ap()
    sp_gate = nc.dram_tensor("sp_gate", [I3, TOK], BF16, kind=dk).ap()
    sp_g2p = nc.dram_tensor("sp_g2p", [I3, TOK], BF16).ap()
    sp_opre = nc.dram_tensor("sp_opre", [I3, TOK], BF16, kind=dk).ap()
    if dbg:
        dbg_att = nc.dram_tensor("dbg_att", [C, TOK], F32, kind="ExternalOutput").ap()
        dbg_sc = nc.dram_tensor("dbg_sc", [128, 16], F32, kind="ExternalOutput").ap()

    es = ExitStack()

    def sb(name, shape, dt=F32):
        return es.enter_context(nc.sbuf_tensor(name, list(shape), dt))

    cs = sb("c_vecs", (128, NC * 8))
    c_lnw = sb("c_lnw", (128, C), BF16)
    c_lnb = sb("c_lnb", (128, C), BF16)
    c_idf = sb("c_idf", (128, 128))
    c_idb = sb("c_idb", (128, 128), BF16)
    c_mask = sb("c_mask", (128, 512), BF16)
    c_ones = sb("c_ones", (128, 128))
    c_cc = sb("c_cc", (128, 2))
    scl = sb("scl", (128, 16))
    mw1 = sb("mw1", (128, NC * 160), BF16)
    mw2 = sb("mw2", (128, 5 * C), BF16)
    tw1 = sb("tw1", (128, NC * 64), BF16)
    tw2 = sb("tw2", (64, C), BF16)
    SS = sb("SS", (128, NC * 128))

    pes = ExitStack()
    pf = [pes.enter_context(nc.psum_tensor("pf%d" % i, [128, 512], F32)) for i in range(6)]
    pb16 = [pes.enter_context(nc.psum_tensor("pb%d" % i, [128, 1024], BF16)) for i in range(2)]

    p = P(nc)

    p.chase_tok = None
    p.gp_chase_tok = None

    def dma_in(dst, src, waits=()):
        sw = [p.chase_tok] if p.chase_tok else []
        p.chase_tok = None
        return p.op(LD, "dma_start", out=dst, in_=src, waits=waits, dma=True,
                    selfw=sw)

    def dma_out(dst, src, waits=(), accum=False):
        sw = [p.gp_chase_tok] if p.gp_chase_tok else []
        p.gp_chase_tok = None
        if accum:
            return p.op(GP, "dma_start", out=dst, in_=src, accum_op=Op.add,
                        waits=waits, dma=True, selfw=sw)
        return p.op(GP, "dma_start", out=dst, in_=src, waits=waits, dma=True,
                    selfw=sw)

    dmy = sb("dmy", (128, 2))
    dmy2 = sb("dmy2", (128, 2))

    def chase():
        t = p.op(LD, "dma_start", out=dmy[:, :], in_=cconst, dma=True,
                 selfw=[("ld", p.cnt["ld"])])
        p.chase_tok = t
        return t

    def gp_chase():
        sw = [("gpd", p.cnt["gpd"])]
        if p.gp_chase_tok:
            sw.append(p.gp_chase_tok)
        p.gp_chase_tok = None
        t = p.op(GP, "dma_start", out=dmy2[:, :], in_=cconst, dma=True,
                 selfw=sw)
        p.gp_chase_tok = t
        return t

    def fm3(dram, col0, ncols):
        return dram[:, col0:col0 + ncols].rearrange("(a p) n -> p a n", p=128)

    def chunked(ap_, k):
        return ap_.rearrange("p (a k) -> p a k", k=k)

    def bwait(key):
        return p.bank.get(key)

    def bset(key, token):
        p.bank[key] = token

    def fin_scale(mxap, mnap, stt, col_s, col_i, extra=()):
        t1 = p.op(DV, "reduce_max", stt[:, 0:1], mxap, axis=AX.X, waits=list(extra),
                  selfw=[p.tok(DV)])
        t1 = p.op(DV, "tensor_reduce", stt[:, 12:13], mnap, AX.X, Op.min)
        t1 = p.op(DV, "tensor_scalar", stt[:, 12:13], stt[:, 12:13], -1.0, None,
                  Op.mult)
        t1 = p.op(DV, "tensor_max", stt[:, 0:1], stt[:, 0:1], stt[:, 12:13])
        t2 = p.op(PE, "transpose", pf[2][0:1, 0:128], stt[:, 0:1], c_idf[:, :],
                  waits=[t1, bwait("b2")])
        t3 = p.op(DV, "reduce_max", stt[0:1, 1:2], pf[2][0:1, 0:128], axis=AX.X,
                  waits=[t2])
        t3 = p.op(DV, "tensor_scalar", stt[0:1, 2:3], stt[0:1, 1:2], 1.0 / QMAX,
                  1e-12, Op.mult, Op.max)
        t3 = p.op(DV, "reciprocal", stt[0:1, 3:4], stt[0:1, 2:3])
        tm = p.op(PE, "matmul", pf[2][:, 0:2], c_ones[0:1, :], stt[0:1, 2:4],
                  start=True, stop=True, waits=[t3])
        t4 = p.op(DV, "tensor_copy", scl[:, col_s:col_s + 1], pf[2][:, 0:1], waits=[tm])
        t4 = p.op(DV, "tensor_copy", scl[:, col_i:col_i + 1], pf[2][:, 1:2])
        bset("b2", t4)
        return t4

    def heron_rstd(stt, c_in, c_out, c_t, c_t2, wide=False, iters=5):
        w = stt[:, c_in:c_in + 1]
        s_ = stt[:, c_out:c_out + 1]
        t_ = stt[:, c_t:c_t + 1]
        t2 = stt[:, c_t2:c_t2 + 1]
        if wide:
            p.op(DV, "tensor_scalar", s_, w, 0.1, 0.95, Op.is_ge, Op.mult)
            p.op(DV, "tensor_scalar", s_, s_, 0.05, None, Op.add)
            p.op(DV, "tensor_scalar", t_, w, 10.0, 29.0, Op.is_ge, Op.mult)
            p.op(DV, "tensor_add", s_, s_, t_)
            p.op(DV, "tensor_scalar", t_, w, 1000.0, 670.0, Op.is_ge, Op.mult)
            p.op(DV, "tensor_add", s_, s_, t_)
        else:
            p.op(DV, "tensor_scalar", s_, w, 0.5, 0.5, Op.mult, Op.add)
        for _ in range(iters):
            p.op(DV, "reciprocal", t_, s_)
            p.op(DV, "tensor_mul", t_, t_, w)
            p.op(DV, "tensor_add", t_, t_, s_)
            p.op(DV, "tensor_scalar", s_, t_, 0.5, None, Op.mult)
        p.op(DV, "reciprocal", t_, s_)
        p.op(DV, "tensor_mul", t2, t_, t_)
        p.op(DV, "tensor_mul", t2, t2, w)
        p.op(DV, "tensor_scalar", t2, t2, -0.5, 1.5, Op.mult, Op.add)
        return p.op(DV, "tensor_mul", s_, t_, t2)

    tcons = [
        dma_in(chunked(cs[:, :], 8), vecs.rearrange("(a p) k -> p a k", p=128)),
        dma_in(c_lnw[:, :], lnwb[:, 0:C]),
        dma_in(c_lnb[:, :], lnwb[:, C:2 * C]),
        dma_in(c_idf[:, :], identf),
        dma_in(c_idb[:, :], identb),
        dma_in(c_mask[:, :], mask4),
        dma_in(c_ones[:, :], onesf),
    ]
    dma_in(c_cc[:, :], cconst)
    tcons = [chase()]
    p.op(DV, "memset", scl[:, :], 1.0)

    if upto == "consts":
        return _finish(nc, p, pes, es)

    with nc.sbuf_tensor("pre_f32", [128, NC * 1024], F32) as stage, \
         nc.sbuf_tensor("pre_b16", [128, NC * 1024], BF16) as stage_b:
        t1 = dma_in(chunked(stage[:, 0:NC * 160], 160),
                    maa_w1.rearrange("(a p) k -> p a k", p=128))
        t2 = p.op(DV, "tensor_copy", mw1[:, :], stage[:, 0:NC * 160],
                  waits=[t1, chase()])
        t1 = dma_in(stage[:, 0:5 * C], maa_w2, waits=[t2])
        t2 = p.op(DV, "tensor_copy", mw2[:, :], stage[:, 0:5 * C],
                  waits=[t1, chase()])
        t1 = dma_in(chunked(stage[:, 0:NC * 64], 64),
                    td_w1.rearrange("(a p) k -> p a k", p=128), waits=[t2])
        t2 = p.op(DV, "tensor_copy", tw1[:, :], stage[:, 0:NC * 64],
                  waits=[t1, chase()])
        t1 = dma_in(stage[0:64, 0:C], td_w2, waits=[t2])
        t2 = p.op(DV, "tensor_copy", tw2[:, :], stage[0:64, 0:C],
                  waits=[t1, chase()])
        tst = t2
        for wi in range(5):
            tl = dma_in(chunked(stage[:, :], 1024),
                        w5[wi * C:(wi + 1) * C, :].rearrange("(a p) n -> p a n", p=128),
                        waits=[tst])
            sc = 0.5 if wi == 4 else 1.0
            tc = p.op(DV, "tensor_scalar", stage_b[:, :], stage[:, :], sc, None,
                      Op.mult, waits=[tl, chase()])
            tst = dma_out(sp_w5[wi * C:(wi + 1) * C, :].rearrange("(a p) n -> p a n", p=128),
                          chunked(stage_b[:, :], 1024), waits=[tc])

        if upto == "w5":
            return _finish(nc, p)

        for dqi, (w_nat, w_t, spill, nko, nblk, bind, bK) in enumerate((
                (wup_n, wupT, sp_wupT, NI, 32, bind_up, 32),
                (wgate_n, wgateT, sp_wgateT, NI, 32, bind_up, 32),
                (wdown_n, wdownT, sp_wdownT, NC, 96, bind_dn, 96))):
            nfree = int(w_nat.shape[1])
            tfree = int(w_t.shape[1])
            ncc = int(w_t.shape[0]) // 128
            with nc.sbuf_tensor("dq_nat%d" % dqi, [128, nfree], F32) as natst, \
                 nc.sbuf_tensor("dq_s%d" % dqi, [128, nko * nblk], F32) as s_nat, \
                 nc.sbuf_tensor("dq_sT%d" % dqi, [bK, nko * 128], F32) as s_T, \
                 nc.sbuf_tensor("dq_bc%d" % dqi, [128, tfree], F32) as s_bc, \
                 nc.sbuf_tensor("dq_ib%d" % dqi, [128, tfree], F32) as i_bc, \
                 nc.sbuf_tensor("dq_bi%d" % dqi, [bK, ncc * 128], F32) as bi, \
                 nc.sbuf_tensor("dq_wt%d" % dqi, [128, tfree], F32) as wtst, \
                 nc.sbuf_tensor("dq_t%d" % dqi, [128, tfree], F32) as tqt, \
                 nc.sbuf_tensor("dq_o%d" % dqi, [128, 2 * tfree], BF16) as ost:
                tb = dma_in(bi[:, :], bind[:, 0:ncc * 128])
                tb = chase()
                tm = p.tok(DV)
                for ko in range(nko):
                    tl = dma_in(natst[:, :], w_nat[ko * 128:(ko + 1) * 128, :], waits=[tm])
                    tm = p.op(DV, "reduce_max", s_nat[:, ko * nblk:(ko + 1) * nblk],
                              natst[:, :].rearrange("p (w k) -> p w k", k=32),
                              axis=AX.X, apply_absolute_value=True,
                              waits=[tl, chase()])
                if upto == "dq_nat" and dqi == 0:
                    return _finish(nc, p)
                ts = p.op(DV, "tensor_scalar", s_nat[:, :], s_nat[:, :], 1.0 / 7.0,
                          1e-12, Op.mult, Op.max, waits=[tm])
                if upto == "dq_ts" and dqi == 0:
                    return _finish(nc, p)
                tev = None
                for ko in range(nko):
                    bk = "b%d" % (ko % 2)
                    tt = p.op(PE, "transpose", pf[ko % 2][0:nblk, 0:128],
                              s_nat[:, ko * nblk:(ko + 1) * nblk], c_idf[:, :],
                              waits=[ts, bwait(bk)])
                    tev = p.op(DV, "tensor_copy", s_T[0:nblk, ko * 128:(ko + 1) * 128],
                               pf[ko % 2][0:nblk, 0:128], waits=[tt])
                    bset(bk, tev)
                tsT = tev
                if upto == "dq_sT" and dqi == 0:
                    return _finish(nc, p)
                tdone = p.tok(GP)
                tmul_h, spch_h = {}, {}
                for cc in range(ncc):
                    tl = dma_in(wtst[:, :], w_t[cc * 128:(cc + 1) * 128, :],
                                waits=[tmul_h.get(cc - 1, tdone)])
                    tl = chase()
                    tbc = None
                    for jb in range(tfree // 512):
                        tmm = p.op(PE, "matmul", pf[1][:, :],
                                   bi[0:bK, cc * 128:(cc + 1) * 128],
                                   s_T[0:bK, jb * 512:(jb + 1) * 512],
                                   start=True, stop=True,
                                   waits=[tsT, tb, bwait("b1")])
                        tbc = p.op(DV, "tensor_copy", s_bc[:, jb * 512:(jb + 1) * 512],
                                   pf[1][:, :], waits=[tmm])
                        bset("b1", tbc)
                    t3 = p.op(DV, "reciprocal", i_bc[:, :], s_bc[:, :], waits=[tbc])
                    t3 = p.op(DV, "tensor_mul", tqt[:, :], wtst[:, :], i_bc[:, :],
                              waits=[tl, t3])
                    tmul_h[cc] = t3
                    t3 = p.op(AC, "activation", tqt[:, :], tqt[:, :], AF.Identity,
                              bias=c_cc[:, 0:1], waits=[t3])
                    t3 = p.op(AC, "activation", tqt[:, :], tqt[:, :], AF.Identity,
                              bias=c_cc[:, 0:1], scale=-1.0)
                    osl = ost[:, (cc % 2) * tfree:(cc % 2) * tfree + tfree]
                    t3 = p.op(DV, "scalar_tensor_tensor", osl, tqt[:, :],
                              -1.0, s_bc[:, :], Op.mult, Op.mult,
                              waits=[t3, spch_h.get(cc - 2)])
                    dma_out(spill[cc * 128:(cc + 1) * 128, :], osl, waits=[t3])
                    spch_h[cc] = gp_chase()
                    if upto == "dq_cc%d" % cc and dqi == 0:
                        return _finish(nc, p)
            p.fence()
            if upto == "dq%d" % dqi:
                return _finish(nc, p)

    p.fence()

    if upto == "pre":
        return _finish(nc, p, pes, es)

    with nc.sbuf_tensor("a_x", [128, 2 * 1024], F32) as a_x, \
         nc.sbuf_tensor("a_sq", [128, 2 * 1024], F32) as a_sq, \
         nc.sbuf_tensor("a_st", [128, 32], F32) as a_st, \
         nc.sbuf_tensor("a_xnf", [128, 2 * 1024], F32) as a_xnf, \
         nc.sbuf_tensor("a_pc", [128, 8], F32) as a_pc, \
         nc.sbuf_tensor("a_xnb", [128, 2 * 1024], BF16) as a_xnb, \
         nc.sbuf_tensor("a_xxb", [128, 2 * 1024], BF16) as a_xxb, \
         nc.sbuf_tensor("a_mx", [128, NST], F32) as a_mx, \
         nc.sbuf_tensor("a_mn", [128, NST], F32) as a_mn:
        ln_cons = {}
        sp_done = {}
        tts = {}
        evs = {}
        minfree = {}

        def a_stats(st):
            bx = (st % 2) * 1024
            xs = a_x[:, bx:bx + 1024]
            sq = a_sq[:, bx:bx + 1024]
            stt = a_st[:, (st % 2) * 16:(st % 2) * 16 + 16]
            tl = dma_in(xs, x_tm[st * 128:(st + 1) * 128, :],
                        waits=[ln_cons.get(st - 2)])
            tl = chase()
            t1 = p.op(DV, "reduce_sum", stt[:, 0:1], xs, axis=AX.X,
                      waits=[tl] + tcons)
            t1 = p.op(DV, "scalar_tensor_tensor", sq, xs, 1.0, xs,
                      Op.bypass, Op.mult, accum_out=stt[:, 1:2],
                      waits=[tts.get(st - 2)])
            p.op(DV, "tensor_scalar", stt[:, 2:3], stt[:, 0:1], 1.0 / 1024.0,
                 None, Op.mult)
            p.op(DV, "tensor_scalar", stt[:, 3:4], stt[:, 1:2], 1.0 / 1024.0,
                 LN_EPS, Op.mult, Op.add, selfw=[t1])
            p.op(DV, "tensor_mul", stt[:, 4:5], stt[:, 2:3], stt[:, 2:3])
            p.op(DV, "tensor_sub", stt[:, 3:4], stt[:, 3:4], stt[:, 4:5])
            heron_rstd(stt, 3, 5, 6, 7, wide=False, iters=5)
            tnrm = p.op(DV, "tensor_scalar", sq, xs, stt[:, 2:3],
                        stt[:, 5:6], Op.subtract, Op.mult)
            ln_cons[st] = tnrm
            tev = tt = None
            for ci in range(NC):
                bk = "b%d" % (ci % 2)
                slot = pf[ci % 2][:, (ci // 2 % 2) * 128:(ci // 2 % 2) * 128 + 128]
                tt = p.op(PE, "transpose", slot, sq[:, ci * 128:(ci + 1) * 128],
                          c_idf[:, :], waits=[tnrm, bwait(bk)])
                tev = p.op(AC, "activation",
                           a_xnf[:, bx + ci * 128:bx + ci * 128 + 128], slot,
                           AF.Identity,
                           waits=[tt, minfree.get(st - 2)] if ci == 0 else [tt])
                bset(bk, tev)
            tts[st] = tt
            evs[st] = tev

        def a_out(st):
            bx = (st % 2) * 1024
            if st % (T // 128) == 0:
                p.op(DV, "memset", a_pc[:, :], 0.0,
                     waits=[sp_done.get(st - 1)])
            xnf3 = chunked(a_xnf[:, bx:bx + 1024], 128)
            xxb3 = chunked(a_xxb[:, bx:bx + 1024], 128)
            t2 = p.op(DV, "tensor_sub", xxb3[:, :, 1:128],
                      xnf3[:, :, 0:127], xnf3[:, :, 1:128],
                      waits=[evs[st], sp_done.get(st - 2)])
            t2 = p.op(DV, "tensor_sub", xxb3[:, :, 0:1],
                      chunked(a_pc[:, :], 1), xnf3[:, :, 0:1])
            p.op(DV, "tensor_copy", chunked(a_pc[:, :], 1), xnf3[:, :, 127:128])
            t3 = p.op(DV, "tensor_scalar", a_xnb[:, bx:bx + 1024],
                      a_xnf[:, bx:bx + 1024],
                      1.0, None, Op.mult, Op.max, accum_out=a_mx[:, st:st + 1])
            minfree[st] = p.op(DV, "tensor_scalar", a_sq[:, bx:bx + 1024],
                               a_xnf[:, bx:bx + 1024], 1.0,
                               None, Op.mult, Op.min, accum_out=a_mn[:, st:st + 1])
            dma_out(fm3(sp_xn, st * 128, 128),
                    chunked(a_xnb[:, bx:bx + 1024], 128), waits=[t3])
            dma_out(fm3(sp_xx, st * 128, 128),
                    chunked(a_xxb[:, bx:bx + 1024], 128), waits=[t2])
            sp_done[st] = gp_chase()

        a_stats(0)
        for st in range(NST):
            if st + 1 < NST:
                a_stats(st + 1)
            a_out(st)
        tsxn = fin_scale(a_mx[:, 0:NST], a_mn[:, 0:NST], a_st, 0, 1)
        tsxn = p.op(DV, "tensor_scalar", scl[:, 12:13], scl[:, 0:1],
                    -CMAGIC, None, Op.mult, waits=[tsxn])
        if dbg:
            p.op(GP, "dma_start", out=dbg_sc, in_=scl[:, :], waits=[tsxn], dma=True)

    p.fence()

    if upto == "A":
        return _finish(nc, p, pes, es)

    pp_es = ExitStack()
    p_xn = pp_es.enter_context(nc.sbuf_tensor("p_xn", [128, 2 * CPT * 1024], BF16))
    p_xx = pp_es.enter_context(nc.sbuf_tensor("p_xx", [128, 2 * CPT * 1024], BF16))
    p_t = pp_es.enter_context(nc.sbuf_tensor("p_t", [128, CPT * 1024], BF16))
    p_mix = pp_es.enter_context(nc.sbuf_tensor("p_mix", [128, CPT * 1024], BF16))
    p_t5a = pp_es.enter_context(nc.sbuf_tensor("p_t5a", [128, TILE], BF16))
    p_t5c = pp_es.enter_context(nc.sbuf_tensor("p_t5c", [64, TILE], BF16))
    p_tdt = pp_es.enter_context(nc.sbuf_tensor("p_tdt", [64, TILE], BF16))
    p_r = pp_es.enter_context(nc.sbuf_tensor("p_r", [128, CPT * 1024], BF16))
    p_k = pp_es.enter_context(nc.sbuf_tensor("p_k", [128, CPT * 1024], BF16))
    p_v = pp_es.enter_context(nc.sbuf_tensor("p_v", [128, CPT * 1024], BF16))
    p_g = pp_es.enter_context(nc.sbuf_tensor("p_g", [128, CPT * 1024], BF16))
    p_th = pp_es.enter_context(nc.sbuf_tensor("p_th", [128, CPT * 1024], BF16))
    p_ww = pp_es.enter_context(nc.sbuf_tensor("p_ww", [128, CPT * 1024], BF16))
    p_c = pp_es.enter_context(nc.sbuf_tensor("p_c", [128, 1024], F32))
    p_s1 = pp_es.enter_context(nc.sbuf_tensor("p_s1", [128, 1024], F32))
    p_s2 = pp_es.enter_context(nc.sbuf_tensor("p_s2", [128, 2 * 1024], F32))
    p_el = pp_es.enter_context(nc.sbuf_tensor("p_el", [128, 16], F32))
    p_rt = pp_es.enter_context(nc.sbuf_tensor("p_rt", [128, 2 * 1024], BF16))
    p_kt = pp_es.enter_context(nc.sbuf_tensor("p_kt", [128, 2 * 1024], BF16))
    p_khtm = pp_es.enter_context(nc.sbuf_tensor("p_khtm", [128, 1024], BF16))
    p_pt = pp_es.enter_context(nc.sbuf_tensor("p_pt", [128, H * 128], BF16))
    p_y = pp_es.enter_context(nc.sbuf_tensor("p_y", [128, 1024], F32))
    p_yn = pp_es.enter_context(nc.sbuf_tensor("p_yn", [128, 1024], BF16))
    p_pd = pp_es.enter_context(nc.sbuf_tensor("p_pd", [128, 16], F32))
    p_gst = pp_es.enter_context(nc.sbuf_tensor("p_gst", [128, 64], F32))
    p_sb = pp_es.enter_context(nc.sbuf_tensor("p_sb", [128, NC * 128], BF16))
    p_z = pp_es.enter_context(nc.sbuf_tensor("p_z", [128, CPT * 1024], BF16))
    p_wst = pp_es.enter_context(nc.sbuf_tensor("p_w", [128, 2 * NC * 1024], BF16))
    p_xr = pp_es.enter_context(nc.sbuf_tensor("p_xr", [128, 512], F32))
    p_os = pp_es.enter_context(nc.sbuf_tensor("p_os", [128, 512], F32))
    if True:

        WIDX = {"r": 0, "k": 1, "v": 2, "g": 3, "o": 4}
        wslot = {}
        wld = {}

        def load_w(name, slot, waits=()):
            wslot[name] = slot
            dma_in(
                chunked(p_wst[:, slot * NC * 1024:(slot + 1) * NC * 1024], 1024),
                sp_w5[WIDX[name] * C:(WIDX[name] + 1) * C, :].rearrange(
                    "(a p) n -> p a n", p=128),
                waits=list(waits))
            wld[name] = chase()

        def wap(name, kc, oc):
            s0 = wslot[name] * NC * 1024 + kc * 1024 + oc * 128
            return p_wst[:, s0:s0 + 128]

        evh = [None, None]
        mixdone = {}
        WKV = {}
        for tl_i in range(NTIL):
            t0 = tl_i * TILE
            sl2 = (tl_i % 2) * CPT * 1024
            xns = p_xn[:, sl2:sl2 + CPT * 1024]
            xxs = p_xx[:, sl2:sl2 + CPT * 1024]
            tld1 = dma_in(chunked(xns, TILE), fm3(sp_xn, t0, TILE),
                          waits=[mixdone.get(tl_i - 2)] if tl_i >= 2
                          else [sp_done[NST - 1], p.tok(DV), p.tok(GP), p.tok(PE)])
            tld2 = dma_in(chunked(xxs, TILE), fm3(sp_xx, t0, TILE))
            tld2 = chase()
            if tl_i % TPS == 0:
                p.op(DV, "memset", SS[:, :], 0.0)
            txxx = None
            for ci in range(NC):
                txxx = p.op(DV, "scalar_tensor_tensor",
                            p_t[:, ci * TILE:(ci + 1) * TILE],
                            xxs[:, ci * TILE:(ci + 1) * TILE],
                            cs[:, ci * 8 + 0:ci * 8 + 1],
                            xns[:, ci * TILE:(ci + 1) * TILE],
                            Op.mult, Op.add, waits=[tld1, tld2])
            tm = tm2 = None
            for kc in range(NC):
                tm = p.op(PE, "matmul", pf[0][0:96, :], mw1[:, kc * 160:kc * 160 + 96],
                          p_t[:, kc * TILE:(kc + 1) * TILE],
                          start=(kc == 0), stop=(kc == NC - 1),
                          waits=[txxx, bwait("b0")])
                tm2 = p.op(PE, "matmul", pf[1][0:64, :],
                           mw1[:, kc * 160 + 96:kc * 160 + 160],
                           p_t[:, kc * TILE:(kc + 1) * TILE],
                           start=(kc == 0), stop=(kc == NC - 1),
                           waits=[bwait("b1")])
            tta = p.op(AC, "activation", p_t5a[0:96, :], pf[0][0:96, :], AF.Tanh,
                       waits=[tm, tm2])
            ttc = p.op(AC, "activation", p_t5c[0:64, :], pf[1][0:64, :], AF.Tanh)
            bset("b0", ttc)
            bset("b1", ttc)
            if upto == "P_maa" and tl_i == 0:
                return _finish(nc, p)

            def emit_mix(f, vcol, waits0):
                tev = None
                for ci in range(NC):
                    if f < 3:
                        rhs = p_t5a[f * 32:(f + 1) * 32, :]
                    else:
                        rhs = p_t5c[(f - 3) * 32:(f - 2) * 32, :]
                    bk = "b%d" % (2 + ci % 2)
                    fb = (f % 3) * 32
                    tmm = p.op(PE, "matmul", pf[2 + ci % 2][:, :],
                               mw2[fb:fb + 32, f * C + ci * 128:f * C + (ci + 1) * 128],
                               rhs, start=True, stop=True,
                               waits=list(waits0) + [bwait(bk)])
                    tev = p.op(AC, "activation", p_t[:, ci * TILE:(ci + 1) * TILE],
                               pf[2 + ci % 2][:, :], AF.Identity,
                               bias=cs[:, ci * 8 + vcol:ci * 8 + vcol + 1],
                               waits=[tmm])
                    bset(bk, tev)
                tq = p.op(DV, "tensor_mul", p_t[:, :], p_t[:, :], xxs,
                          waits=[tev])
                tmix = p.op(DV, "tensor_add", p_mix[:, :], p_t[:, :], xns)
                return tmix

            def emit_proj(dst, wname, waits0, act_g=False):
                tev = None
                for oc in range(NC):
                    bk = "b%d" % (4 + oc % 2)
                    bank = pf[4 + oc % 2]
                    tmm = None
                    for kc in range(NC):
                        tmm = p.op(PE, "matmul", bank[:, :], wap(wname, kc, oc),
                                   p_mix[:, kc * TILE:(kc + 1) * TILE],
                                   start=(kc == 0), stop=(kc == NC - 1),
                                   waits=(list(waits0) + [wld[wname], bwait(bk)])
                                   if kc == 0 else ())
                    if act_g:
                        tev = p.op(AC, "activation", p_g[:, oc * TILE:(oc + 1) * TILE],
                                   bank[:, :], AF.Identity, waits=[tmm])
                        tev = p.op(AC, "activation", p_th[:, oc * TILE:(oc + 1) * TILE],
                                   bank[:, :], AF.Tanh, scale=0.5)
                    else:
                        tev = p.op(DV, "tensor_copy", dst[:, oc * TILE:(oc + 1) * TILE],
                                   bank[:, :], waits=[tmm])
                    bset(bk, tev)
                return tev

            tmw = emit_mix(0, 1, [tta, ttc])
            tdm = None
            for kc in range(NC):
                tdm = p.op(PE, "matmul", pf[3][0:64, :], tw1[:, kc * 64:(kc + 1) * 64],
                           p_mix[:, kc * TILE:(kc + 1) * TILE],
                           start=(kc == 0), stop=(kc == NC - 1),
                           waits=[tmw, bwait("b3")] if kc == 0 else ())
            ttd = p.op(AC, "activation", p_tdt[0:64, :], pf[3][0:64, :], AF.Tanh,
                       waits=[tdm])
            bset("b3", ttd)
            tww = None
            for oc in range(NC):
                twm = p.op(PE, "matmul", pf[2][:, :],
                           tw2[0:64, oc * 128:(oc + 1) * 128], p_tdt[0:64, :],
                           start=True, stop=True, waits=[ttd, bwait("b2")])
                tww = p.op(AC, "activation", p_ww[:, oc * TILE:(oc + 1) * TILE],
                           pf[2][:, :], AF.Exp, bias=cs[:, oc * 8 + 6:oc * 8 + 7],
                           waits=[twm])
                bset("b2", tww)
            if upto == "P_w" and tl_i == 0:
                return _finish(nc, p)

            tmk = emit_mix(1, 2, [tmw])
            load_w("k", 0, waits=[p.tok(PE)] if tl_i else ())
            tpk = emit_proj(p_k, "k", [tmk])
            tmr = emit_mix(3, 4, [tmk])
            if upto == "P_kmix" and tl_i == 0:
                return _finish(nc, p)
            load_w("r", 1, waits=())
            tpr = emit_proj(p_r, "r", [tmr])
            if upto == "P_kr" and tl_i == 0:
                return _finish(nc, p)
            tmv = emit_mix(2, 3, [tmr])
            load_w("v", 0, waits=[tpk, p.tok(PE)])
            tpv = None
            for ts_ in range(CPT):
                for oh in range(2):
                    bk = "b%d" % (4 + (ts_ * 2 + oh) % 2)
                    bank = pf[4 + (ts_ * 2 + oh) % 2]
                    tmm = None
                    for kc in range(NC):
                        s0 = 0 * NC * 1024 + kc * 1024 + oh * 512
                        tmm = p.op(PE, "matmul", bank[:, :],
                                   p_mix[:, kc * TILE + ts_ * 128:kc * TILE + ts_ * 128 + 128],
                                   p_wst[:, s0:s0 + 512],
                                   start=(kc == 0), stop=(kc == NC - 1),
                                   waits=[tmv, wld["v"], bwait(bk)] if kc == 0 else ())
                    tpv = p.op(DV, "tensor_copy",
                               p_v[:, ts_ * 1024 + oh * 512:ts_ * 1024 + oh * 512 + 512],
                               bank[:, :], waits=[tmm])
                    bset(bk, tpv)
            tmg = emit_mix(4, 5, [tmv])
            mixdone[tl_i] = tmg
            load_w("g", 1, waits=[tpr, p.tok(PE)])
            tpg = emit_proj(None, "g", [tmg], act_g=True)
            load_w("o", 0, waits=[tpg, p.tok(PE)])
            tgf = p.op(DV, "scalar_tensor_tensor", p_g[:, :], p_th[:, :], 1.0,
                       p_g[:, :], Op.add, Op.mult, waits=[tpg])
            if upto == "P_g" and tl_i == 0:
                return _finish(nc, p)

            p_sq = p_t[:, 0:2048].bitcast(F32)
            quads = [[0, 2, 4, 6], [8, 10, 12, 14], [1, 3, 5, 7], [9, 11, 13, 15]]

            def wkv_head(j):
                jj = tl_i * CPT + j
                cs0 = j * 128
                sl = jj % 2
                rsl_r = chunked(p_r[:, :], TILE)[:, :, cs0:cs0 + 128]
                rsl_k = chunked(p_k[:, :], TILE)[:, :, cs0:cs0 + 128]
                rsl_w = chunked(p_ww[:, :], TILE)[:, :, cs0:cs0 + 128]
                s2 = p_s2[:, sl * 1024:sl * 1024 + 1024]
                rt = p_rt[:, sl * 1024:sl * 1024 + 1024]
                kt = p_kt[:, sl * 1024:sl * 1024 + 1024]
                el = p_el[:, sl * 8:sl * 8 + 8]
                tsc = None
                for ci in range(NC):
                    tsc = p.op(DV, "tensor_tensor_scan",
                               p_c[:, ci * 128:(ci + 1) * 128],
                               p_ww[:, ci * TILE + cs0:ci * TILE + cs0 + 128],
                               p_ww[:, ci * TILE + cs0:ci * TILE + cs0 + 128],
                               0.0, Op.add, Op.bypass,
                               waits=[tww, WKV.get(("te2", jj - 1))]
                               if ci == 0 else ())
                tel = p.op(AC, "activation", el,
                           p_c[:, 127::128].rearrange("p (a o) -> p a o", o=1),
                           AF.Exp, scale=-1.0,
                           waits=[tsc, WKV.get(("tsu", jj - 2))])
                tcx = p.op(DV, "tensor_sub", chunked(p_s1[:, :], 128),
                           chunked(p_c[:, :], 128), rsl_w)
                te1 = p.op(AC, "activation", p_s1[:, :], p_s1[:, :], AF.Exp,
                           scale=-1.0, waits=[tcx, WKV.get(("trt", jj - 1))])
                te2 = p.op(AC, "activation", s2, p_c[:, :], AF.Exp,
                           waits=[WKV.get(("tpd", jj - 2))])
                trt = p.op(DV, "tensor_mul", chunked(rt, 128), rsl_r,
                           chunked(p_s1[:, :], 128),
                           waits=[te1, WKV.get(("ty15", jj - 2))])
                tkt = p.op(DV, "tensor_mul", chunked(kt, 128), rsl_k,
                           chunked(s2, 128),
                           waits=[te2, WKV.get(("tkb", jj - 2))])
                truk = None
                for ci in range(NC):
                    truk = p.op(DV, "scalar_tensor_tensor",
                                s2[:, ci * 128:(ci + 1) * 128],
                                rsl_r[:, ci, :], cs[:, ci * 8 + 7:ci * 8 + 8],
                                rsl_k[:, ci, :], Op.mult, Op.mult)
                WKV[("te2", jj)] = te2
                WKV[("tel", jj)] = tel
                WKV[("trt", jj)] = trt
                WKV[("tkt", jj)] = tkt
                WKV[("truk", jj)] = truk

            wkv_head(0)
            for j in range(CPT):
                jj = tl_i * CPT + j
                cs0 = j * 128
                sl = jj % 2
                rsl_g = chunked(p_g[:, :], TILE)[:, :, cs0:cs0 + 128]
                s2 = p_s2[:, sl * 1024:sl * 1024 + 1024]
                rt = p_rt[:, sl * 1024:sl * 1024 + 1024]
                kt = p_kt[:, sl * 1024:sl * 1024 + 1024]
                el = p_el[:, sl * 8:sl * 8 + 8]
                tpd = None
                for hh in [h2 * 2 for h2 in range(8)] + [h2 * 2 + 1 for h2 in range(8)]:
                    ci, hb = hh // 2, (hh % 2) * 64
                    tpd = p.op(PE, "matmul", pf[3][:, hh:hh + 1],
                               s2[hb:hb + 64, ci * 128:(ci + 1) * 128],
                               c_ones[hb:hb + 64, 0:1], start=True, stop=True,
                               waits=[WKV[("truk", jj)], bwait("b3")]
                               if hh == 0 else ())
                WKV[("tpd", jj)] = tpd
                tpdc = p.op(DV, "tensor_copy", p_pd[:, :], pf[3][:, 0:16], waits=[tpd])
                bset("b3", tpdc)
                tmaskd = {}
                for gi, quad in enumerate(quads):
                    bk = "b%d" % (gi % 2)
                    bank = pf[gi % 2]
                    tg4 = None
                    for hq, hh in enumerate(quad):
                        ci, hb = hh // 2, (hh % 2) * 64
                        tg4 = p.op(PE, "matmul", bank[:, hq * 128:(hq + 1) * 128],
                                   kt[hb:hb + 64, ci * 128:(ci + 1) * 128],
                                   rt[hb:hb + 64, ci * 128:(ci + 1) * 128],
                                   start=True, stop=True,
                                   waits=[WKV[("trt", jj)], WKV[("tkt", jj)],
                                          bwait(bk)] if hq == 0 else ())
                    ptdst = p_pt[:, quad[0] * 128:quad[0] * 128 + 896]
                    ptdst = ptdst.rearrange("p (a k) -> p a k", k=128)[:, 0::2, :]
                    tmk_ = p.op(DV, "tensor_mul", ptdst,
                                bank[:, :].rearrange("p (a k) -> p a k", k=128),
                                c_mask[:, :].rearrange("p (a k) -> p a k", k=128),
                                waits=[tg4])
                    for hh in quad:
                        tmaskd[hh] = tmk_
                    bset(bk, tmk_)
                tlastpt = tg4
                tkh = None
                for ci in range(NC):
                    tkh = p.op(DV, "tensor_scalar", kt[:, ci * 128:(ci + 1) * 128],
                               kt[:, ci * 128:(ci + 1) * 128],
                               el[:, ci:ci + 1], None, Op.mult,
                               waits=[tlastpt, WKV[("tel", jj)]] if ci == 0 else ())
                tkb = None
                for ci in range(NC):
                    tkb = p.op(PE, "transpose", pb16[0][:, ci * 128:(ci + 1) * 128],
                               kt[:, ci * 128:(ci + 1) * 128], c_idb[:, :],
                               waits=[tkh, bwait("pb0")] if ci == 0 else ())
                WKV[("tkb", jj)] = tkb
                tktm = p.op(DV, "tensor_copy", p_khtm[:, :], pb16[0][:, :], waits=[tkb])
                bset("pb0", tktm)
                tsb = p.op(DV, "tensor_copy", p_sb[:, :], SS[:, :])
                if j + 1 < CPT:
                    wkv_head(j + 1)
                tylast = None
                for ci in range(NC):
                    bk = "b%d" % (2 + ci // 4)
                    bank = pf[2 + ci // 4]
                    slc = (ci % 4) * 128
                    p.op(PE, "matmul", bank[:, slc:slc + 128],
                         rt[:, ci * 128:(ci + 1) * 128],
                         p_sb[:, ci * 128:(ci + 1) * 128],
                         start=True, stop=False,
                         waits=[tmaskd[2 * ci], tmaskd[2 * ci + 1], tsb, bwait(bk)])
                    for hh in (2 * ci, 2 * ci + 1):
                        tylast = p.op(PE, "matmul",
                                      bank[:, slc + (hh % 2) * 64:slc + (hh % 2) * 64 + 64],
                                      p_pt[:, hh * 128:(hh + 1) * 128],
                                      p_v[:, j * 1024 + hh * 64:j * 1024 + hh * 64 + 64],
                                      start=False, stop=(hh % 2 == 1))
                WKV[("ty15", jj)] = tylast
                tyl = None
                for hh in range(H):
                    bank = pf[2 + hh // 8]
                    tyl = p.op(DV, "scalar_tensor_tensor",
                               p_y[:, hh * 64:(hh + 1) * 64],
                               p_v[:, j * 1024 + hh * 64:j * 1024 + hh * 64 + 64],
                               p_pd[:, hh:hh + 1],
                               bank[:, (hh % 8) * 64:(hh % 8) * 64 + 64],
                               Op.mult, Op.add,
                               waits=[tylast, tpdc])
                    bset("b%d" % (2 + hh // 8), tyl)
                tsu = None
                for ci in range(NC):
                    bk = "b%d" % (4 + ci % 2)
                    bank = pf[4 + ci % 2]
                    tu_ = p.op(PE, "matmul", bank[:, 0:128],
                               p_khtm[:, ci * 128:(ci + 1) * 128],
                               p_v[:, j * 1024 + ci * 128:j * 1024 + ci * 128 + 128],
                               start=True, stop=True,
                               waits=[tktm, bwait(bk)])
                    for hb in (0, 64):
                        tsu = p.op(DV, "scalar_tensor_tensor",
                                   SS[hb:hb + 64, ci * 128 + hb:ci * 128 + hb + 64],
                                   SS[hb:hb + 64, ci * 128 + hb:ci * 128 + hb + 64],
                                   el[hb:hb + 64, ci:ci + 1],
                                   bank[hb:hb + 64, hb:hb + 64],
                                   Op.mult, Op.add,
                                   waits=[tu_, tsb] if hb == 0 else ())
                    bset(bk, tsu)
                WKV[("tsu", jj)] = tsu
                p.op(DV, "reduce_sum", p_gst[:, 0:16],
                     p_y[:, :].rearrange("p (w k) -> p w k", k=64), axis=AX.X,
                     waits=[tyl])
                tsq = p.op(AC, "activation", p_sq[:, :], p_y[:, :], AF.Square,
                           waits=[tyl, WKV.get(("zdone", jj - 1))])
                p.op(DV, "reduce_sum", p_gst[:, 16:32],
                     p_sq[:, :].rearrange("p (w k) -> p w k", k=64), axis=AX.X,
                     waits=[tsq])
                p.op(DV, "tensor_scalar", p_gst[:, 0:16], p_gst[:, 0:16],
                     1.0 / 64.0, None, Op.mult)
                p.op(DV, "tensor_scalar", p_gst[:, 16:32], p_gst[:, 16:32],
                     1.0 / 64.0, GN_EPS, Op.mult, Op.add)
                p.op(DV, "tensor_mul", p_gst[:, 32:48], p_gst[:, 0:16], p_gst[:, 0:16])
                p.op(DV, "tensor_sub", p_gst[:, 16:32], p_gst[:, 16:32],
                     p_gst[:, 32:48])
                gsl = lambda c0: p_gst[:, c0:c0 + 16]
                wv, sv, tv = gsl(16), gsl(32), gsl(48)
                p.op(DV, "tensor_scalar", sv, wv, 0.1, 0.95, Op.is_ge, Op.mult)
                p.op(DV, "tensor_scalar", sv, sv, 0.05, None, Op.add)
                p.op(DV, "tensor_scalar", tv, wv, 10.0, 29.0, Op.is_ge, Op.mult)
                p.op(DV, "tensor_add", sv, sv, tv)
                p.op(DV, "tensor_scalar", tv, wv, 1000.0, 670.0, Op.is_ge, Op.mult)
                p.op(DV, "tensor_add", sv, sv, tv)
                for _ in range(6):
                    p.op(DV, "reciprocal", tv, sv)
                    p.op(DV, "tensor_mul", tv, tv, wv)
                    p.op(DV, "tensor_add", tv, tv, sv)
                    p.op(DV, "tensor_scalar", sv, tv, 0.5, None, Op.mult)
                p.op(DV, "reciprocal", tv, sv)
                p.op(DV, "tensor_mul", sv, tv, tv)
                p.op(DV, "tensor_mul", sv, sv, wv)
                p.op(DV, "tensor_scalar", sv, sv, -0.5, 1.5, Op.mult, Op.add)
                p.op(DV, "tensor_mul", sv, tv, sv)
                tgn = None
                for hh in range(H):
                    tgn = p.op(DV, "tensor_scalar", p_yn[:, hh * 64:(hh + 1) * 64],
                               p_y[:, hh * 64:(hh + 1) * 64],
                               p_gst[:, hh:hh + 1], p_gst[:, 32 + hh:33 + hh],
                               Op.subtract, Op.mult)
                tz = p.op(DV, "tensor_mul", p_sq[:, :], p_yn[:, :], c_lnw[:, :],
                          waits=[tgn, tsq])
                tz = p.op(DV, "tensor_add", p_sq[:, :], p_sq[:, :], c_lnb[:, :])
                tz = p.op(DV, "tensor_copy", p_yn[:, :], p_sq[:, :])
                WKV[("zdone", jj)] = tz
                tzt = None
                for ci in range(NC):
                    tzt = p.op(PE, "transpose", pb16[1][:, ci * 128:(ci + 1) * 128],
                               p_yn[:, ci * 128:(ci + 1) * 128], c_idb[:, :],
                               waits=[tz, bwait("pb1")] if ci == 0 else ())
                prev_z = p.op(DV, "tensor_mul",
                              chunked(p_z[:, :], TILE)[:, :, cs0:cs0 + 128],
                              chunked(pb16[1][:, :], 128), rsl_g,
                              waits=[tzt, tgf])
                bset("pb1", prev_z)
            if upto == "P_z" and tl_i == 0:
                return _finish(nc, p)
            for oc in (reversed(range(NC)) if dbgout == "ocrev" else range(NC)):
                if dbgout == "mmb2":
                    bk = "b%d" % (2 + oc % 2)
                    bank = pf[2 + oc % 2]
                else:
                    bk = "b%d" % (4 + oc % 2)
                    bank = pf[4 + oc % 2]
                tmo = None
                for kc in range(NC):
                    tmo = p.op(PE, "matmul", bank[:, :], wap("o", kc, oc),
                               p_z[:, kc * TILE:(kc + 1) * TILE],
                               start=(kc == 0), stop=(kc == NC - 1),
                               waits=[prev_z, wld["o"], bwait(bk)] if kc == 0 else ())
                xr = p_xr[:, :]
                txl = dma_in(xr, x_fm[oc * 128:(oc + 1) * 128, t0:t0 + TILE],
                             waits=[evh[0]])
                txl = chase()
                tout = p.op(DV, "tensor_add", p_os[:, :], bank[:, :], xr,
                            waits=[tmo, txl, bwait("pos")])
                evh[0] = tout
                bset(bk, tout)
                bset("pos", dma_out(out_fm[oc * 128:(oc + 1) * 128, t0:t0 + TILE],
                                    p_os[:, :], waits=[tout]))

    pp_es.close()
    p.fence()

    if upto == "P":
        return _finish(nc, p, pes, es)

    with nc.sbuf_tensor("m_wu", [128, NC * I3], BF16) as m_wu, \
         nc.sbuf_tensor("m_wg", [128, NC * I3], BF16) as m_wg, \
         nc.sbuf_tensor("m_xn", [128, 2 * NC * 512], BF16) as m_xn, \
         nc.sbuf_tensor("m_tq", [128, NC * 512], F32) as m_tq, \
         nc.sbuf_tensor("m_xq", [128, 2 * NC * 512], BF16) as m_xq, \
         nc.sbuf_tensor("m_ob", [128, 8 * 512], BF16) as m_ob, \
         nc.sbuf_tensor("m_mx", [128, 2 * NI * 16], F32) as m_mx, \
         nc.sbuf_tensor("m_mn", [128, 2 * NI * 16], F32) as m_mn, \
         nc.sbuf_tensor("m_sc", [128, 512], F32) as m_sc, \
         nc.sbuf_tensor("m_st", [128, 16], F32) as m_st:
        p.op(DV, "memset", m_mx[:, :], 0.0)
        p.op(DV, "memset", m_mn[:, :], 0.0)
        twu = dma_in(chunked(m_wu[:, :], I3), sp_wupT.rearrange("(a p) n -> p a n", p=128))
        twg = dma_in(chunked(m_wg[:, :], I3), sp_wgateT.rearrange("(a p) n -> p a n", p=128))
        twu = twg = chase()
        slot_done = {}
        q1h, mmh = {}, {}
        for tl_i in range(NTIL):
            t0 = tl_i * TILE
            sl = (tl_i % 2) * NC * 512
            tld = dma_in(chunked(m_xn[:, sl:sl + NC * 512], 512), fm3(sp_xn, t0, TILE),
                         waits=[q1h.get(tl_i - 2)] if tl_i >= 2
                         else [p.tok(DV), p.tok(AC), p.tok(GP)])
            tld = chase()
            tq1 = p.op(AC, "activation", m_tq[:, :], m_xn[:, sl:sl + NC * 512],
                       AF.Identity, bias=c_cc[:, 0:1], scale=scl[:, 1:2],
                       waits=[tld, tsxn])
            q1h[tl_i] = tq1
            tq2 = p.op(AC, "activation", m_xq[:, sl:sl + NC * 512], m_tq[:, :],
                       AF.Identity, bias=scl[:, 12:13], scale=scl[:, 0:1],
                       waits=[mmh.get(tl_i - 2)])
            for iw, (wsb, spo, mxo, tw) in enumerate(
                    ((m_wu, sp_up, 0, twu), (m_wg, sp_gate, 1, twg))):
                for oc in range(NI):
                    slot = iw * 4 + oc % 4
                    bk = "b%d" % (iw * 2 + oc % 2)
                    bank = pf[iw * 2 + oc % 2]
                    tmm = None
                    for kc in range(NC):
                        tmm = p.op(PE, "matmul", bank[:, :],
                                   wsb[:, kc * I3 + oc * 128:kc * I3 + oc * 128 + 128],
                                   m_xq[:, sl + kc * 512:sl + (kc + 1) * 512],
                                   start=(kc == 0), stop=(kc == NC - 1),
                                   waits=[tq2, tw, bwait(bk)] if kc == 0 else ())
                    mmh[tl_i] = tmm
                    mcol = mxo * NI * 16 + oc * 16 + tl_i
                    tev = p.op(DV, "tensor_scalar",
                               m_ob[:, slot * 512:(slot + 1) * 512], bank[:, :],
                               1.0, None, Op.mult, Op.max,
                               accum_out=m_mx[:, mcol:mcol + 1],
                               waits=[tmm, slot_done.get(slot)])
                    bset(bk, tev)
                    p.op(DV, "tensor_scalar", m_sc[:, :],
                         m_ob[:, slot * 512:(slot + 1) * 512], 1.0, None,
                         Op.mult, Op.min, accum_out=m_mn[:, mcol:mcol + 1])
                    dma_out(
                        spo[oc * 128:(oc + 1) * 128, t0:t0 + TILE],
                        m_ob[:, slot * 512:(slot + 1) * 512], waits=[tev])
                    if oc % 4 == 3:
                        tch4 = gp_chase()
                        for q_ in range(4):
                            slot_done[iw * 4 + q_] = tch4
        tlast = [slot_done[s] for s in slot_done]
        fin_scale(m_mx[:, 0:NI * 16], m_mn[:, 0:NI * 16], m_st, 2, 3, extra=[p.tok(DV)])
        tsg = fin_scale(m_mx[:, NI * 16:2 * NI * 16], m_mn[:, NI * 16:2 * NI * 16], m_st, 4, 5)
        tsg = p.op(DV, "tensor_scalar", scl[:, 10:11], scl[:, 4:5], 0.5, None,
                   Op.mult, waits=[tsg])
        tsg = p.op(DV, "tensor_scalar", scl[:, 13:14], scl[:, 10:11],
                   -CMAGIC, None, Op.mult, waits=[tsg])

    p.fence()

    if upto == "M":
        return _finish(nc, p, pes, es)

    I3C = NI * 128
    with nc.sbuf_tensor("e_g", [128, 2 * I3C], BF16) as e_g, \
         nc.sbuf_tensor("e_t1", [128, 2 * I3C], F32) as e_t1, \
         nc.sbuf_tensor("e_t2", [128, 2 * I3C], F32) as e_t2, \
         nc.sbuf_tensor("e_t3", [128, 2 * I3C], F32) as e_t3, \
         nc.sbuf_tensor("e_o", [128, 2 * I3C], BF16) as e_o, \
         nc.sbuf_tensor("e_mx", [128, NST], F32) as e_mx, \
         nc.sbuf_tensor("e_mn", [128, NST], F32) as e_mn, \
         nc.sbuf_tensor("e_st", [128, 16], F32) as e_st:
        p.op(DV, "memset", e_mn[:, :], 0.0)
        q1h, d3h, d4h, sch = {}, {}, {}, {}
        for et in range(NST):
            t0 = et * 128
            s0 = (et % 2) * I3C
            g_ = e_g[:, s0:s0 + I3C]
            t1 = e_t1[:, s0:s0 + I3C]
            t2 = e_t2[:, s0:s0 + I3C]
            t3 = e_t3[:, s0:s0 + I3C]
            o_ = e_o[:, s0:s0 + I3C]
            tld = dma_in(chunked(g_, 128), fm3(sp_gate, t0, 128),
                         waits=[q1h.get(et - 2)] if et >= 2
                         else [p.tok(DV), p.tok(AC), p.tok(GP)])
            tld = chase()
            q1h[et] = p.op(AC, "activation", t1, g_, AF.Identity,
                           bias=c_cc[:, 0:1], scale=scl[:, 5:6],
                           waits=[tld, tsg, d3h.get(et - 2)])
            p.op(AC, "activation", t2, t1, AF.Identity,
                 bias=scl[:, 13:14], scale=scl[:, 10:11],
                 waits=[d4h.get(et - 2)])
            p.op(AC, "activation", t3, t2, AF.Tanh, waits=[d4h.get(et - 2)])
            q4 = p.op(AC, "activation", t1, t3, AF.Identity,
                      bias=c_cc[:, 1:2], scale=32768.0)
            d3h[et] = p.op(DV, "tensor_scalar", t3, t1, c_cc[:, 0:1],
                           65535.0, Op.subtract, Op.min, waits=[q4])
            d4o = p.op(DV, "scalar_tensor_tensor", o_, t2,
                       2.0 / 65536.0, t3, Op.mult, Op.mult,
                       waits=[sch.get(et - 2)])
            p.op(DV, "tensor_scalar", t3, o_, 1.0, None,
                 Op.mult, Op.max, accum_out=e_mx[:, et:et + 1])
            d4h[et] = p.op(DV, "tensor_scalar", t3, o_, 1.0, None,
                           Op.mult, Op.min, accum_out=e_mn[:, et:et + 1])
            dma_out(fm3(sp_g2p, t0, 128), chunked(o_, 128), waits=[d4o])
            sch[et] = gp_chase()
        ts3 = fin_scale(e_mx[:, :], e_mn[:, :], e_st, 6, 7,
                        extra=[p.tok(DV), p.tok(GP)])
        ts3 = p.op(DV, "tensor_mul", scl[:, 11:12], scl[:, 2:3], scl[:, 6:7],
                   waits=[ts3])
        ts3 = p.op(DV, "tensor_scalar", scl[:, 14:15], scl[:, 11:12],
                   -CMAGIC, None, Op.mult, waits=[ts3])

    p.fence()

    if upto == "E1":
        return _finish(nc, p, pes, es)

    with nc.sbuf_tensor("f_g", [128, 2 * I3C], BF16) as f_g, \
         nc.sbuf_tensor("f_u", [128, 2 * I3C], BF16) as f_u, \
         nc.sbuf_tensor("f_t1", [128, 2 * I3C], F32) as f_t1, \
         nc.sbuf_tensor("f_t2", [128, 2 * I3C], F32) as f_t2, \
         nc.sbuf_tensor("f_o", [128, 2 * I3C], BF16) as f_o, \
         nc.sbuf_tensor("f_mx", [128, NST], F32) as f_mx, \
         nc.sbuf_tensor("f_mn", [128, NST], F32) as f_mn, \
         nc.sbuf_tensor("f_st", [128, 16], F32) as f_st:
        p.op(DV, "memset", f_mn[:, :], 0.0)
        q1h, d2h, sch = {}, {}, {}
        for et in range(NST):
            t0 = et * 128
            s0 = (et % 2) * I3C
            g_ = f_g[:, s0:s0 + I3C]
            u_ = f_u[:, s0:s0 + I3C]
            t1 = f_t1[:, s0:s0 + I3C]
            t2 = f_t2[:, s0:s0 + I3C]
            o_ = f_o[:, s0:s0 + I3C]
            dma_in(chunked(g_, 128), fm3(sp_g2p, t0, 128),
                   waits=[q1h.get(et - 2)] if et >= 2
                   else [p.tok(DV), p.tok(AC), p.tok(GP)])
            dma_in(chunked(u_, 128), fm3(sp_up, t0, 128))
            tld = chase()
            p.op(AC, "activation", t1, g_, AF.Identity,
                 bias=c_cc[:, 0:1], scale=scl[:, 7:8],
                 waits=[tld, ts3, d2h.get(et - 2)])
            q1h[et] = p.op(AC, "activation", t2, u_, AF.Identity,
                           bias=c_cc[:, 0:1], scale=scl[:, 3:4],
                           waits=[d2h.get(et - 2)])
            q3 = p.op(AC, "activation", t2, t2, AF.Identity,
                      bias=scl[:, 14:15], scale=scl[:, 11:12])
            d2o = p.op(DV, "scalar_tensor_tensor", o_, t1,
                       c_cc[:, 0:1], t2, Op.subtract, Op.mult,
                       waits=[q3, sch.get(et - 2)])
            p.op(DV, "tensor_scalar", t1, o_, 1.0, None,
                 Op.mult, Op.max, accum_out=f_mx[:, et:et + 1])
            d2h[et] = p.op(DV, "tensor_scalar", t1, o_, 1.0, None,
                           Op.mult, Op.min, accum_out=f_mn[:, et:et + 1])
            dma_out(fm3(sp_opre, t0, 128), chunked(o_, 128), waits=[d2o])
            sch[et] = gp_chase()
        ts4 = fin_scale(f_mx[:, :], f_mn[:, :], f_st, 8, 9,
                        extra=[p.tok(DV), p.tok(GP)])
        ts4 = p.op(DV, "tensor_scalar", scl[:, 15:16], scl[:, 8:9],
                   -CMAGIC, None, Op.mult, waits=[ts4])

    p.fence()

    if upto == "E2":
        return _finish(nc, p, pes, es)

    _gp_after_P = p.cnt["gpd"]

    def _gpfence(h, v=_gp_after_P, s=p.sem["gpd"]):
        h.wait_ge(s, v)
    p.items[GP].append(_gpfence)

    with nc.sbuf_tensor("g_wd", [128, NI * C], BF16) as g_wd, \
         nc.sbuf_tensor("g_op", [128, 2 * NI * 512], BF16) as g_op, \
         nc.sbuf_tensor("g_tq", [128, 2 * 512], F32) as g_tq, \
         nc.sbuf_tensor("g_ob", [128, NI * 512], BF16) as g_ob, \
         nc.sbuf_tensor("g_fs", [128, 2 * 512], F32) as g_fs:
        twd = dma_in(chunked(g_wd[:, :], C), sp_wdownT.rearrange("(a p) n -> p a n", p=128))
        twd = chase()
        qh, mmic = {}, {}
        for tl_i in range(NTIL):
            t0 = tl_i * TILE
            s0 = (tl_i % 2) * NI * 512
            tld = dma_in(chunked(g_op[:, s0:s0 + NI * 512], 512),
                         fm3(sp_opre, t0, TILE),
                         waits=[qh.get(tl_i - 2)] if tl_i >= 2
                         else [p.tok(DV), p.tok(AC), p.tok(GP)])
            tld = chase()
            tqq = None
            for ic in range(NI):
                ta = p.op(AC, "activation", g_tq[:, (ic % 2) * 512:(ic % 2) * 512 + 512],
                          g_op[:, s0 + ic * 512:s0 + ic * 512 + 512], AF.Identity,
                          bias=c_cc[:, 0:1], scale=scl[:, 9:10],
                          waits=[tld, ts4] if ic == 0 else ())
                tqq = p.op(AC, "activation", g_ob[:, ic * 512:(ic + 1) * 512],
                           g_tq[:, (ic % 2) * 512:(ic % 2) * 512 + 512], AF.Identity,
                           bias=scl[:, 15:16], scale=scl[:, 8:9],
                           waits=[mmic.get((tl_i - 1, ic))])
            qh[tl_i] = ta
            for hc in range(NC):
                bk = "b%d" % (4 + hc % 2)
                bank = pf[4 + hc % 2]
                tmm = None
                for ic in range(NI):
                    tmm = p.op(PE, "matmul", bank[:, :],
                               g_wd[:, ic * C + hc * 128:ic * C + hc * 128 + 128],
                               g_ob[:, ic * 512:(ic + 1) * 512],
                               start=(ic == 0), stop=(ic == NI - 1),
                               waits=[tqq, twd, bwait(bk)] if ic == 0 else ())
                    if hc == NC - 1:
                        mmic[(tl_i, ic)] = tmm
                tev = p.op(DV, "tensor_copy", g_fs[:, (hc % 2) * 512:(hc % 2) * 512 + 512],
                           bank[:, :], waits=[tmm, bwait("gfs%d" % (hc % 2))])
                bset(bk, tev)
                dma_out(out_fm[hc * 128:(hc + 1) * 128, t0:t0 + TILE],
                        g_fs[:, (hc % 2) * 512:(hc % 2) * 512 + 512],
                        waits=[tev], accum=True)
                bset("gfs%d" % (hc % 2), gp_chase())

    fgp = p.cnt["gpd"]

    def fin(h, fgp=fgp, s=p.sem["gpd"]):
        h.wait_ge(s, fgp)
    p.items[GP].append(fin)

    with nc.Block() as block:
        p.finalize(block)
    pes.close()
    es.close()
    return nc


def _host_inputs(inputs, core, bpc=4, T=2048):
    TOK = bpc * T
    x = np.asarray(inputs["L0_attn_x"][core * bpc:(core + 1) * bpc, :T], np.float32)
    x2 = np.ascontiguousarray(x.reshape(TOK, C))
    x_fm = np.ascontiguousarray(x2.T)
    w5 = np.concatenate([np.asarray(inputs["L0_attn_W_r"], np.float32),
                         np.asarray(inputs["L0_attn_W_k"], np.float32),
                         np.asarray(inputs["L0_attn_W_v"], np.float32),
                         np.asarray(inputs["L0_attn_W_g"], np.float32),
                         np.asarray(inputs["L0_attn_W_o"], np.float32)], axis=0)
    maa2 = np.ascontiguousarray(np.tile(np.asarray(
        inputs["L0_attn_time_maa_w2"], np.float32).transpose(1, 0, 2).reshape(32, 5 * C),
        (4, 1)))
    vecs = np.stack([
        np.asarray(inputs["L0_attn_time_maa_x"], np.float32).reshape(C),
        np.asarray(inputs["L0_attn_time_maa_w"], np.float32).reshape(C),
        np.asarray(inputs["L0_attn_time_maa_k"], np.float32).reshape(C),
        np.asarray(inputs["L0_attn_time_maa_v"], np.float32).reshape(C),
        np.asarray(inputs["L0_attn_time_maa_r"], np.float32).reshape(C),
        np.asarray(inputs["L0_attn_time_maa_g"], np.float32).reshape(C),
        np.asarray(inputs["L0_attn_time_decay"], np.float32).reshape(C),
        np.asarray(inputs["L0_attn_time_faaaa"], np.float32).reshape(C)],
        axis=1).astype(np.float32)
    lnwb = np.concatenate([
        np.tile(np.asarray(inputs["L0_attn_ln_w"], np.float32).reshape(1, C), (128, 1)),
        np.tile(np.asarray(inputs["L0_attn_ln_b"], np.float32).reshape(1, C), (128, 1))],
        axis=1)
    bind_up = np.zeros((32, NC * 128), np.float32)
    for cc in range(NC):
        for pp in range(128):
            bind_up[cc * 4 + pp // 32, cc * 128 + pp] = 1.0
    bind_dn = np.zeros((96, NI * 128), np.float32)
    for cc in range(NI):
        for pp in range(128):
            bind_dn[cc * 4 + pp // 32, cc * 128 + pp] = 1.0
    return {
        "x_tm": x2, "x_fm": x_fm, "w5": w5,
        "maa_w1": np.asarray(inputs["L0_attn_time_maa_w1"], np.float32),
        "maa_w2": maa2,
        "td_w1": np.asarray(inputs["L0_attn_time_decay_w1"], np.float32),
        "td_w2": np.asarray(inputs["L0_attn_time_decay_w2"], np.float32),
        "vecs": vecs, "lnwb": lnwb.astype(ml_dtypes.bfloat16),
        "identf": np.eye(128, dtype=np.float32),
        "identb": np.eye(128, dtype=ml_dtypes.bfloat16),
        "mask4": np.tile(np.triu(np.ones((128, 128), np.float32), 1),
                         (1, 4)).astype(ml_dtypes.bfloat16),
        "onesf": np.ones((128, 128), np.float32),
        "bind_up": bind_up, "bind_dn": bind_dn,
        "wup_n": np.asarray(inputs["L0_ffn_w_up"], np.float32),
        "wgate_n": np.asarray(inputs["L0_ffn_w_gate"], np.float32),
        "wdown_n": np.asarray(inputs["L0_ffn_w_down"], np.float32),
        "wupT": np.ascontiguousarray(np.asarray(inputs["L0_ffn_w_up"], np.float32).T),
        "wgateT": np.ascontiguousarray(np.asarray(inputs["L0_ffn_w_gate"], np.float32).T),
        "wdownT": np.ascontiguousarray(np.asarray(inputs["L0_ffn_w_down"], np.float32).T),
        "cconst": np.tile(np.array([[CMAGIC, CMAGIC + 32768.0]], np.float32), (128, 1)),
    }


def kernel(**inputs):
    bpc = 32 // NCORES
    nc = build_program(bpc=bpc, T=2048)
    in_maps = [_host_inputs(inputs, c, bpc=bpc) for c in range(NCORES)]
    res = run_bass_kernel_spmd(nc, in_maps, core_ids=list(range(NCORES)))
    outs = []
    for c in range(NCORES):
        o = np.asarray(res.results[c]["out_fm"])
        outs.append(np.ascontiguousarray(o.T).reshape(bpc, 2048, C))
    return np.concatenate(outs, axis=0)
